# version 10
# baseline (speedup 1.0000x reference)
import os
import numpy as np
from contextlib import ExitStack

import concourse.bass as bass
import concourse.mybir as mybir
from concourse.bass_utils import run_bass_kernel_spmd

F32 = mybir.dt.float32
BF16 = mybir.dt.bfloat16
AF = mybir.ActivationFunctionType
ALU = mybir.AluOpType
AX = mybir.AxisListType

S, Dm, NH, HA, FF, EX = 2048, 1024, 16, 64, 4096, 8
NSC, NDC, NHC, NTC, NFC, NFB, NTT, NG = 16, 8, 8, 16, 32, 4, 4, 4

ENGN = ("tensor", "vector", "scalar", "sync", "gpsimd")


class Ctr:
    __slots__ = ("sem", "n")

    def __init__(self, nc, name):
        self.sem = nc.alloc_semaphore(name=name)
        self.n = 0


class Sched:
    def __init__(self, prev_inc=None):
        self.ops = {e: [] for e in ENGN}
        self.prev_inc = prev_inc if prev_inc is not None else {}

    def add(self, eng, emit, waits=(), incs=()):
        waits = list(waits)
        pi = self.prev_inc.get(eng)
        if pi is not None:
            ctr0, v0 = pi
            if not any(c is ctr0 and v >= v0 for c, v in waits):
                waits.insert(0, (ctr0, v0))
        rincs = []
        for ctr, k in incs:
            ctr.n += k
            rincs.append((ctr, k))
        if rincs:
            self.prev_inc[eng] = (rincs[-1][0], rincs[-1][0].n)
        self.ops[eng].append((tuple(waits), emit, rincs))

    def run(self, nc):
        with nc.Block() as blk:
            for name in ENGN:
                ops = self.ops[name]
                if not ops:
                    continue

                def make(ops_):
                    def _f(eng):
                        for waits, emit, incs in ops_:
                            for ctr, v in waits:
                                eng.wait_ge(ctr.sem, v)
                            inst = emit(eng)
                            for ctr, k in incs:
                                inst.then_inc(ctr.sem, k)
                    return _f

                getattr(blk, name)(make(ops))


K_PHASES = int(os.environ.get("K_PHASES", "9"))


def build_nc():
    nc = bass.Bass("TRN2", target_bir_lowering=False)

    x_h = nc.dram_tensor("x", [S, Dm], F32, kind="ExternalInput")
    wq_h = nc.dram_tensor("wq", [NH, Dm, HA], F32, kind="ExternalInput")
    wk_h = nc.dram_tensor("wk", [NH, Dm, HA], F32, kind="ExternalInput")
    wv_h = nc.dram_tensor("wv", [NH, Dm, HA], F32, kind="ExternalInput")
    wo_h = nc.dram_tensor("wo", [NH, HA, Dm], F32, kind="ExternalInput")
    bq_h = nc.dram_tensor("bq", [NH, HA], F32, kind="ExternalInput")
    bk_h = nc.dram_tensor("bk", [NH, HA], F32, kind="ExternalInput")
    bv_h = nc.dram_tensor("bv", [NH, HA], F32, kind="ExternalInput")
    bo_h = nc.dram_tensor("bo", [1, Dm], F32, kind="ExternalInput")
    wr_h = nc.dram_tensor("wr", [Dm, EX], F32, kind="ExternalInput")
    w1_h = nc.dram_tensor("w1", [EX, Dm, FF], F32, kind="ExternalInput")
    b1_h = nc.dram_tensor("b1", [EX, FF], F32, kind="ExternalInput")
    w2_h = nc.dram_tensor("w2", [EX, FF, Dm], F32, kind="ExternalInput")
    b2_h = nc.dram_tensor("b2", [EX, Dm], F32, kind="ExternalInput")
    out_h = nc.dram_tensor("out", [S, Dm], F32, kind="ExternalOutput")
    w1b_h = nc.dram_tensor("w1b", [EX, Dm, FF], BF16, kind="Internal")
    w2b_h = nc.dram_tensor("w2b", [EX, FF, Dm], BF16, kind="Internal")

    ctx = ExitStack()

    AR_BYTES = 204736
    ar = ctx.enter_context(nc.sbuf_tensor("arena", [128, AR_BYTES], mybir.dt.uint8))

    def V(off, dtype, *dims):
        n = int(np.prod(dims)) * mybir.dt.size(dtype)
        assert off % 64 == 0 and off + n <= AR_BYTES, (off, n)
        ap = ar[:, off:off + n].bitcast(dtype)
        if len(dims) > 1:
            names = " ".join(f"d{i}" for i in range(len(dims)))
            kw = {f"d{i}": dims[i] for i in range(len(dims) - 1)}
            ap = ap.rearrange(f"p ({names}) -> p {names}", **kw)
        return ap

    O_IDB, O_IDF, O_MADD = 0, 256, 768
    O_BQS, O_BKS, O_B1S, O_WRB = 1280, 1344, 1408, 2432
    O_PROBS, O_M, O_SM = 2560, 2624, 2752
    O_STX, O_STW, O_STW2 = 4800, 12992, 21184
    O_STXB, O_S1 = 29376, 33472
    D0 = 41664
    O_QT, O_KT, O_VT = D0, D0 + 32768, D0 + 65536
    O_XT0, O_WSL = D0 + 98304, D0 + 131072
    O_PT, O_ZT = D0 + 98304, D0 + 114688
    O_PBF, O_PSB = O_STX, O_STW2
    O_WO, O_X1 = D0 + 98304, D0
    O_XT1, O_HT, O_ACC, O_W1B = D0 + 65536, D0 + 73728, D0 + 106496, D0 + 122880
    O_W2B = O_STXB
    assert D0 + 155648 <= AR_BYTES

    identb = V(O_IDB, BF16, 128)
    identf = V(O_IDF, F32, 128)
    madd = V(O_MADD, F32, 128)
    bq_s = V(O_BQS, F32, 8)
    bk_s = V(O_BKS, F32, 8)
    b1s = V(O_B1S, F32, EX, 32)
    wr_bf = V(O_WRB, BF16, 8, 8)
    probs = V(O_PROBS, F32, 8)
    Mw = V(O_M, F32, 4, 8)

    def SM(i):
        return V(O_SM + 64 * i, F32, 1)

    smax, ssum, srr, pmax = SM(0), SM(1), SM(2), SM(3)
    lsum, lsq, lmean, lvar = SM(4), SM(5), SM(6), SM(7)
    lrstd, lnmb, epsb, lm2, llog = SM(8), SM(9), SM(10), SM(11), SM(12)
    smax_s = [SM(16 + i) for i in range(4)]
    ssum_s = [SM(20 + i) for i in range(4)]
    srr_s = [SM(24 + i) for i in range(4)]

    xstage = V(O_STX, F32, 2, 1024)
    wstage = V(O_STW, F32, 4, 1024)
    w2stage = V(O_STX, F32, 2, 1024)
    xbstage = V(O_STXB, BF16, 2, 1024)
    s1buf = V(O_S1, F32, 2, 1024)

    QT = V(O_QT, BF16, 8, 2048)
    KT = V(O_KT, BF16, 8, 2048)
    Vtm = V(O_VT, BF16, 16, 1024)
    XT0 = V(O_XT0, BF16, 8, 2048)
    Wslot = V(O_WSL, BF16, 8, 1024)
    PTg = V(O_PT, BF16, 16, 512)
    ZT = V(O_ZT, BF16, 8, 2048)
    _PSZ = [1664, 1792, 1920, 2048]
    P_sb_s = [V(D0 + 147456, F32, 1664), V(O_S1, F32, 1792),
              V(O_STW, F32, 1920), V(O_STW2, F32, 2048)]
    P_bf_s = [V(201408, BF16, 1664), V(O_STXB, BF16, 1792),
              V(O_STX + 4096, BF16, 1920), V(O_STX, BF16, 2048)]
    Wo_bf = V(O_WO, BF16, 8, 1024)
    x1 = V(O_X1, F32, 16, 1024)
    XT1 = V(O_XT1, BF16, 8, 512)
    HT = V(O_HT, BF16, 32, 512)
    acc = V(O_ACC, F32, 4, 1024)
    W1bf = V(O_W1B, BF16, 2, 8, 1024)
    W1R = V(O_W1B, BF16, 4, 8, 512)
    whf = V(O_STW, F32, 8, 512)
    W2bf = V(O_W2B, BF16, 2, 1024)
    w2b2 = V(O_S1, F32, 2, 1024)
    W2bf2 = V(D0 + 155648, BF16, 2, 1024)
    cv_buf = [V(197312, BF16, 1024), V(199360, BF16, 1024)]

    ones_row = ctx.enter_context(nc.sbuf_tensor("ones_row", [1, 128], BF16))
    bv_row = ctx.enter_context(nc.sbuf_tensor("bv_row", [1, Dm], BF16))
    bo_row = ctx.enter_context(nc.sbuf_tensor("bo_row", [1, Dm], BF16))
    b2s = ctx.enter_context(nc.sbuf_tensor("b2s", [EX, Dm], BF16))
    MT_bf = ctx.enter_context(nc.sbuf_tensor("MT_bf", [EX, 512], BF16))
    bstage = ctx.enter_context(nc.sbuf_tensor("bstage", [32, 128], F32))

    cXS = [Ctr(nc, "cXS0"), Ctr(nc, "cXS1")]
    cWS = [Ctr(nc, "cWS0"), Ctr(nc, "cWS1"), Ctr(nc, "cWS2"), Ctr(nc, "cWS3")]
    cWF = [Ctr(nc, f"cWF{i}") for i in range(8)]
    cW2R = [Ctr(nc, f"cW2R{i}") for i in range(4)]
    cCVI = [Ctr(nc, f"cCVI{i}") for i in range(4)]
    cCVO = [Ctr(nc, f"cCVO{i}") for i in range(4)]
    cW2S = [Ctr(nc, "cW2S0"), Ctr(nc, "cW2S1")]
    cBS = Ctr(nc, "cBS")
    cOUTs = [Ctr(nc, "cOUT0"), Ctr(nc, "cOUT1"), Ctr(nc, "cOUT2"), Ctr(nc, "cOUT3")]
    cGP = Ctr(nc, "cGP")
    cPE = Ctr(nc, "cPE")
    cVE = Ctr(nc, "cVE")
    cSA = Ctr(nc, "cSA")
    cGC = Ctr(nc, "cGC")
    cOUT = Ctr(nc, "cOUT")


    def _emit_dbg():
        dbg_name = os.environ.get("K_DBG", "")
        if not dbg_name:
            return
        dbg_h = nc.dram_tensor("dbg", [128, 16384], F32, kind="ExternalOutput")
        region = {
            "x1": x1[:, :, :].rearrange("p a b -> p (a b)"),
            "psb": P_sb[:, :],
            "acc": acc[:, :, :].rearrange("p a b -> p (a b)"),
            "qt": QT[:, :, :].rearrange("p a b -> p (a b)"),
            "kt": KT[:, :, :].rearrange("p a b -> p (a b)"),
            "vt": Vtm[:, :, :].rearrange("p a b -> p (a b)"),
            "zt": ZT[:, :, :].rearrange("p a b -> p (a b)"),
            "xt0": XT0[:, :, :].rearrange("p a b -> p (a b)"),
            "xt1": XT1[:, :, :].rearrange("p a b -> p (a b)"),
            "ht": HT[:, :, :].rearrange("p a b -> p (a b)"),
            "pbf": P_bf[:, :],
            "ptg": PTg[:, :, :].rearrange("p a b -> p (a b)"),
            "m": Mw[:, :, :].rearrange("p a b -> p (a b)"),
        }[dbg_name]
        with nc.Block() as blk:
            @blk.gpsimd
            def _(g):
                g.dma_start(dbg_h[:, 0:region.free_size()],
                            region).then_inc(cGP.sem, 16)
                g.wait_ge(cGP.sem, cGP.n + 16)
        cGP.n += 16

    w1b_p = w1b_h.ap().rearrange("e (dc p) f -> e p dc f", p=128)
    x_t = x_h.ap().rearrange("(n p) d -> n p d", p=128)
    out_t = out_h.ap().rearrange("(n p) d -> n p d", p=128)
    wq_p = wq_h.ap().rearrange("h d a -> d h a")
    wk_p = wk_h.ap().rearrange("h d a -> d h a")
    wv_p = wv_h.ap().rearrange("h d a -> d h a")
    wo_p = wo_h.ap().rearrange("h a d -> (h a) d")
    wr_p = wr_h.ap().rearrange("(c p) e -> p c e", p=128)
    bq_p = bq_h.ap().rearrange("(g y) a -> g (y a)", g=8)
    bk_p = bk_h.ap().rearrange("(g y) a -> g (y a)", g=8)
    bv_p = bv_h.ap().rearrange("h a -> (h a)").unsqueeze(0)
    b1_p = b1_h.ap().rearrange("e (p c) -> e p c", p=32)

    _prev_inc = {}

    ps0 = ExitStack()
    ps_xa = ps0.enter_context(nc.psum_tensor("ps_xa", [128, 512], BF16))
    ps_xb = ps0.enter_context(nc.psum_tensor("ps_xb", [128, 512], BF16))
    ps_bias = ps0.enter_context(nc.psum_tensor("ps_bias", [128, 32], F32))

    sc = Sched(_prev_inc)

    def g_consts0(g):
        g.memset(identb[:, :], 0.0)
        g.memset(identf[:, :], 0.0)
        g.memset(madd[:, :], 0.0)
        g.memset(ones_row[:, :], 1.0)
        return g.memset(epsb[:, :], 1e-5)

    sc.add("gpsimd", g_consts0, incs=[(cGC, 1)])
    _gc0 = cGC.n

    def g_consts1(g):
        g.affine_select(identb[:, :], identb[:, :], [[-1, 128]], ALU.not_equal,
                        1.0, base=0, channel_multiplier=1)
        g.affine_select(identf[:, :], identf[:, :], [[-1, 128]], ALU.not_equal,
                        1.0, base=0, channel_multiplier=1)
        return g.affine_select(madd[:, :], madd[:, :], [[-1, 128]], ALU.is_ge,
                               -100000.0, base=0, channel_multiplier=1)

    sc.add("gpsimd", g_consts1, waits=[(cGC, _gc0)], incs=[(cGC, 1)])
    sc.add("gpsimd", lambda g: g.dma_start(bv_row[:, :], bv_p), incs=[(cGP, 16)])
    sc.add("gpsimd", lambda g: g.dma_start(bo_row[:, :], bo_h[0:1, :]), incs=[(cGP, 16)])
    sc.add("gpsimd", lambda g: g.dma_start(b2s[:, :], b2_h[:, :]), incs=[(cGP, 16)])

    xdma_at, xcast_at, xtr_evac_at = {}, {}, {}

    def emit_xdma(tc):
        w = [(cVE, xcast_at[tc - 2])] if tc >= 2 else []
        sc.add("sync", (lambda t2: lambda s: s.dma_start(
            xstage[:, t2 % 2, :], x_t[t2]))(tc), waits=w,
            incs=[(cXS[tc % 2], 16)])
        xdma_at[tc] = cXS[tc % 2].n

    emit_xdma(0)
    emit_xdma(1)
    for tc in range(NTC):
        w = [(cXS[tc % 2], xdma_at[tc])]
        if tc >= 2:
            w.append((cSA, xtr_evac_at[(tc - 2, 1)]))
        sc.add("vector", (lambda t2: lambda v: v.tensor_copy(
            xbstage[:, t2 % 2, :], xstage[:, t2 % 2, :]))(tc),
            waits=w, incs=[(cVE, 1)])
        xcast_at[tc] = cVE.n
        if tc + 2 < NTC:
            emit_xdma(tc + 2)
        for pack in range(2):
            ps = ps_xa if pack == 0 else ps_xb
            w = [(cVE, xcast_at[tc]), (cGC, 2)]
            if tc > 0:
                w.append((cSA, xtr_evac_at[(tc - 1, pack)]))

            def mk_tr(t2, p2, ps2):
                def f(t):
                    ii = None
                    for j in range(4):
                        dc = p2 * 4 + j
                        ii = t.transpose(ps2[:, j * 128:(j + 1) * 128],
                                         xbstage[:, t2 % 2, dc * 128:(dc + 1) * 128],
                                         identb[:, :])
                    return ii
                return f

            sc.add("tensor", mk_tr(tc, pack, ps), waits=w, incs=[(cPE, 1)])
            pe_mark = cPE.n

            def mk_ev(t2, p2, ps2):
                def f(s):
                    return s.copy(
                        XT0[:, p2 * 4:(p2 + 1) * 4, t2 * 128:(t2 + 1) * 128],
                        ps2[:, :].rearrange("p (a b) -> p a b", a=4))
                return f

            sc.add("scalar", mk_ev(tc, pack, ps), waits=[(cPE, pe_mark)],
                   incs=[(cSA, 1)])
            xtr_evac_at[(tc, pack)] = cSA.n

    bias_jobs = [
        (bq_p, 8, lambda s: s.activation(bq_s[:, :], ps_bias[:, 0:8], AF.Identity,
                                         bias=0.0, scale=0.125)),
        (bk_p, 8, lambda s: s.activation(bk_s[:, :], ps_bias[:, 0:8], AF.Identity,
                                         bias=0.0, scale=1.0)),
    ]
    for e_ in range(EX):
        bias_jobs.append((b1_p[e_], 32, (lambda e2: lambda s: s.activation(
            b1s[:, e2, :], ps_bias[:, 0:32], AF.Identity, bias=0.0, scale=1.0))(e_)))
    prev_tr, prev_ev = None, None
    for j, (src, rows, evac) in enumerate(bias_jobs):
        w = [(cPE, prev_tr)] if prev_tr is not None else []
        sc.add("sync", (lambda s2, r2: lambda s: s.dma_start(
            bstage[0:r2, :], s2))(src, rows), waits=w, incs=[(cBS, 16)])
        w = [(cBS, cBS.n), (cGC, 2)]
        if prev_ev is not None:
            w.append((cSA, prev_ev))
        sc.add("tensor", (lambda r2: lambda t: t.transpose(
            ps_bias[:, 0:r2], bstage[0:r2, :], identf[0:r2, 0:r2]))(rows),
            waits=w, incs=[(cPE, 1)])
        prev_tr = cPE.n
        sc.add("scalar", evac, waits=[(cPE, cPE.n)], incs=[(cSA, 1)])
        prev_ev = cSA.n

    sc.add("sync", lambda s: s.dma_start(
        wstage[:, 0, 0:64].rearrange("p (c e) -> p c e", c=8), wr_p),
        incs=[(cWS[0], 16)])
    sc.add("vector", lambda v: v.tensor_copy(
        wr_bf[:, :, :].rearrange("p c e -> p (c e)"), wstage[:, 0, 0:64]),
        waits=[(cWS[0], cWS[0].n)], incs=[(cVE, 1)])
    sc.run(nc)
    ps0.close()

    if K_PHASES < 1:
        _emit_dbg()
        ctx.close()
        return nc
    ps1 = ExitStack()
    ps_qk = ps1.enter_context(nc.psum_tensor("ps_qk", [128, 4, 512], F32))
    ps_v = ps1.enter_context(nc.psum_tensor("ps_v", [128, 1024], F32))

    sc = Sched(_prev_inc)
    wl_pe_done = []
    bank_last_evac = {}

    def wload(widx, wsrc):
        cast_at = []
        for dc in range(NDC):
            w = []
            if dc >= 4:
                w.append((cVE, cast_at[dc - 4]))
            if dc < 4 and widx > 0:
                w.append((cPE, wl_pe_done[widx - 1]))
            sc.add("sync", (lambda w2, d2: lambda s: s.dma_start(
                wstage[:, d2 % 4, :].rearrange("p (h a) -> p h a", h=16),
                w2[d2 * 128:(d2 + 1) * 128]))(wsrc, dc),
                waits=w, incs=[(cWS[dc % 4], 16)])
            w2_ = [(cWS[dc % 4], cWS[dc % 4].n)]
            if widx > 0:
                w2_.append((cPE, wl_pe_done[widx - 1]))
            sc.add("vector", (lambda d2: lambda v: v.tensor_copy(
                Wslot[:, d2, :], wstage[:, d2 % 4, :]))(dc),
                waits=w2_, incs=[(cVE, 1)])
            cast_at.append(cVE.n)
        return cast_at

    for widx, (wsrc, qt, scale, bias_ap) in enumerate(
            [(wq_p, QT, 0.125, bq_s), (wk_p, KT, 1.0, bk_s)]):
        cast_at = wload(widx, wsrc)
        for c in range(NHC):
            for st in range(4):
                w = [(cVE, cast_at[-1])]
                if st in bank_last_evac:
                    w.append((cSA, bank_last_evac[st]))

                def mk_mm(c2, st2):
                    def f(t):
                        ii = None
                        for dc in range(NDC):
                            ii = t.matmul(ps_qk[:, st2, :],
                                          Wslot[:, dc, c2 * 128:(c2 + 1) * 128],
                                          XT0[:, dc, st2 * 512:(st2 + 1) * 512],
                                          start=(dc == 0), stop=(dc == NDC - 1))
                        return ii
                    return f

                sc.add("tensor", mk_mm(c, st), waits=w, incs=[(cPE, 1)])

                def mk_ev(c2, st2, qt2, sc2, b2_):
                    def f(s):
                        return s.activation(
                            qt2[:, c2, st2 * 512:(st2 + 1) * 512],
                            ps_qk[:, st2, :], AF.Identity,
                            bias=b2_[:, c2:c2 + 1], scale=sc2)
                    return f

                sc.add("scalar", mk_ev(c, st, qt, scale, bias_ap),
                       waits=[(cPE, cPE.n)], incs=[(cSA, 1)])
                bank_last_evac[st] = cSA.n
        wl_pe_done.append(cPE.n)

    cast_at = wload(2, wv_p)
    v_evac_prev = None
    for tc in range(NTC):
        w = [(cVE, cast_at[-1])]
        if tc == 0:
            w.append((cGP, 48))
        if v_evac_prev is not None:
            w.append((cVE, v_evac_prev))

        def mk_vmm(t2):
            def f(t):
                ii = None
                for half in range(2):
                    sl = ps_v[:, half * 512:(half + 1) * 512]
                    for dc in range(NDC):
                        ii = t.matmul(sl, XT0[:, dc, t2 * 128:(t2 + 1) * 128],
                                      Wslot[:, dc, half * 512:(half + 1) * 512],
                                      start=(dc == 0), stop=False)
                    ii = t.matmul(sl, ones_row[:, :],
                                  bv_row[:, half * 512:(half + 1) * 512],
                                  start=False, stop=True)
                return ii
            return f

        sc.add("tensor", mk_vmm(tc), waits=w, incs=[(cPE, 1)])
        sc.add("vector", (lambda t2: lambda v: v.tensor_copy(
            Vtm[:, t2, :], ps_v[:, :]))(tc), waits=[(cPE, cPE.n)],
            incs=[(cVE, 1)])
        v_evac_prev = cVE.n
    sc.run(nc)
    ps1.close()

    if K_PHASES < 2:
        _emit_dbg()
        ctx.close()
        return nc
    ps2 = ExitStack()
    ps_sc = ps2.enter_context(nc.psum_tensor("ps_sc", [128, 2048], F32))
    ps_ta = ps2.enter_context(nc.psum_tensor("ps_ta", [128, 512], BF16))
    ps_tb = ps2.enter_context(nc.psum_tensor("ps_tb", [128, 512], BF16))
    ps_zt = ps2.enter_context(nc.psum_tensor("ps_zt", [128, 512], F32))

    sc = Sched(_prev_inc)

    gp_ms_at = []
    for g in range(NG):
        def mk_ms(g2):
            def f(gp):
                ii = None
                for slot in range(4):
                    sca = 4 * g2 + slot
                    for tb in range(sca + 1, 4 * g2 + 4):
                        ii = gp.memset(PTg[:, tb, slot * 128:(slot + 1) * 128], 0.0)
                assert ii is not None
                return ii
            return f
        sc.add("gpsimd", mk_ms(g), incs=[(cGC, 1)])
        gp_ms_at.append(cGC.n)

    evF_at, evD_at, mx_at, exp_at, nm_at = {}, {}, {}, {}, {}
    trpk_at, trev_at, scores_at = {}, {}, {}
    zt_ev_prev = [None]
    pack_hist = {}

    def prev_key(g, h, s_):
        if s_ > 0:
            return (g, h, s_ - 1)
        fh = g * 16 + h
        if fh == 0:
            return None
        pg, ph = divmod(fh - 1, 16)
        return (pg, ph, 3)

    def emit_scores(g, h, s_):
        t_used = (4 * g + s_ + 1) * 128
        c, p0 = h // 2, (h % 2) * 64
        w = []
        pk = prev_key(g, h, s_)
        if pk is not None:
            w.append((cSA, evF_at[pk]))
            w.append((cVE, evD_at[pk]))

        def f(t):
            ii = None
            for tcol in range((t_used + 511) // 512):
                lo = tcol * 512
                hi = min(t_used, lo + 512)
                ii = t.matmul(
                    ps_sc[:, lo:hi],
                    QT[p0:p0 + 64, c, (4 * g + s_) * 128:(4 * g + s_ + 1) * 128],
                    KT[p0:p0 + 64, c, lo:hi],
                    start=True, stop=True,
                    tile_position=((p0, 0) if p0 else None))
            return ii

        sc.add("tensor", f, waits=w, incs=[(cPE, 1)])
        scores_at[(g, h, s_)] = cPE.n

        sfull = (4 * g + s_) * 128
        psb = P_sb_s[s_]
        sc.add("vector", (lambda sf, pb2: lambda v: v.tensor_add(
            pb2[0:128, sf:sf + 128], ps_sc[:, sf:sf + 128], madd[:, :]))(sfull, psb),
            waits=[(cPE, scores_at[(g, h, s_)])], incs=[(cVE, 1)])
        evD_at[(g, h, s_)] = cVE.n

        if sfull > 0:
            w_ = [(cPE, scores_at[(g, h, s_)]), (cVE, evD_at[(g, h, s_)])]
            pk = prev_key(g, h, s_) if s_ > 0 else None
            fh = g * 16 + h
            if fh > 0:
                pg, ph = divmod(fh - 1, 16)
                w_.append((cVE, mx_at[(pg, ph, s_)]))
            sc.add("scalar", (lambda sf, pb2: lambda s: s.copy(
                pb2[0:128, 0:sf], ps_sc[:, 0:sf]))(sfull, psb),
                waits=w_, incs=[(cSA, 1)])
        evF_at[(g, h, s_)] = cSA.n

    def emit_max(g, h, s_):
        t_used = (4 * g + s_ + 1) * 128
        sc.add("vector", (lambda tu, s2: lambda v: v.reduce_max(
            smax_s[s2][:, :], P_sb_s[s2][0:128, 0:tu], AX.X,
            negate=True))(t_used, s_),
            waits=[(cSA, evF_at[(g, h, s_)]), (cVE, evD_at[(g, h, s_)])],
            incs=[(cVE, 1)])
        mx_at[(g, h, s_)] = cVE.n

    def emit_exp(g, h, s_):
        t_used = (4 * g + s_ + 1) * 128
        fh = g * 16 + h
        w = [(cVE, mx_at[(g, h, s_)])]
        if fh > 0:
            pg, ph = divmod(fh - 1, 16)
            w.append((cPE, trpk_at[(pg, ph, s_)]))
        pb = P_bf_s[s_]
        sc.add("scalar", (lambda tu, pb2, s2: lambda s: s.activation(
            pb2[0:128, 0:tu], P_sb_s[s2][0:128, 0:tu], AF.Exp,
            bias=smax_s[s2][:, 0:1], scale=1.0,
            accum_out=ssum_s[s2][:, :]))(t_used, pb, s_),
            waits=w, incs=[(cSA, 1)])
        exp_at[(g, h, s_)] = cSA.n

    def emit_norm(g, h, s_):
        t_used = (4 * g + s_ + 1) * 128
        pb = P_bf_s[s_]
        sc.add("vector", (lambda s2: lambda v: v.reciprocal(
            srr_s[s2][:, :], ssum_s[s2][:, :]))(s_),
            waits=[(cSA, exp_at[(g, h, s_)])], incs=[(cVE, 1)])
        sc.add("vector", (lambda tu, pb2, s2: lambda v: v.tensor_scalar_mul(
            pb2[0:128, 0:tu], pb2[0:128, 0:tu], srr_s[s2][:, 0:1]))(t_used, pb, s_),
            incs=[(cVE, 1)])
        nm_at[(g, h, s_)] = cVE.n

    def emit_transposes(g, h, s_):
        sc_abs = 4 * g + s_
        slot = s_
        pb = P_bf_s[s_]
        packs = [(tb0, min(4, sc_abs + 1 - tb0)) for tb0 in range(0, sc_abs + 1, 4)]
        last_pe = None
        ev_marks = []
        for pi, (tb0, cnt) in enumerate(packs):
            ps = ps_ta if pi % 2 == 0 else ps_tb
            w = [(cVE, nm_at[(g, h, s_)])]
            pu = pack_hist.get(id(ps))
            if pu is not None:
                w.append(pu)

            def mk_tr(tb0_, cnt_, ps_, pb2):
                def f(t):
                    ii = None
                    for j in range(cnt_):
                        ii = t.transpose(
                            ps_[:, j * 128:(j + 1) * 128],
                            pb2[0:128, (tb0_ + j) * 128:(tb0_ + j + 1) * 128],
                            identb[:, :])
                    return ii
                return f

            sc.add("tensor", mk_tr(tb0, cnt, ps, pb), waits=w, incs=[(cPE, 1)])
            last_pe = cPE.n

            def mk_ev(tb0_, cnt_, ps_, sl_):
                def f(eng):
                    return eng.copy(
                        PTg[:, tb0_:tb0_ + cnt_, sl_ * 128:(sl_ + 1) * 128],
                        ps_[:, 0:cnt_ * 128].rearrange("p (a b) -> p a b", a=cnt_))                         if hasattr(eng, 'activation') else eng.tensor_copy(
                        PTg[:, tb0_:tb0_ + cnt_, sl_ * 128:(sl_ + 1) * 128],
                        ps_[:, 0:cnt_ * 128].rearrange("p (a b) -> p a b", a=cnt_))
                return f

            if pi % 2 == 0:
                sc.add("scalar", mk_ev(tb0, cnt, ps, slot),
                       waits=[(cPE, last_pe)], incs=[(cSA, 1)])
                pack_hist[id(ps)] = (cSA, cSA.n)
                ev_marks.append((cSA, cSA.n))
            else:
                sc.add("vector", mk_ev(tb0, cnt, ps, slot),
                       waits=[(cPE, last_pe)], incs=[(cVE, 1)])
                pack_hist[id(ps)] = (cVE, cVE.n)
                ev_marks.append((cVE, cVE.n))
        trpk_at[(g, h, s_)] = last_pe
        trev_at[(g, h, s_)] = ev_marks

    def emit_av(g, h):
        ntb = 4 * g + 4
        p0 = (h % 2) * 64
        c = h // 2
        w = list(trev_at[(g, h, 3)]) + [(cGC, gp_ms_at[g])]
        for s2_ in range(3):
            w += list(trev_at[(g, h, s2_)])
        if zt_ev_prev[0] is not None:
            w.append((cSA, zt_ev_prev[0]))

        def f(t):
            ii = None
            for tb in range(ntb):
                ii = t.matmul(ps_zt[p0:p0 + 64, :],
                              Vtm[:, tb, h * 64:(h + 1) * 64],
                              PTg[:, tb, :],
                              start=(tb == 0), stop=(tb == ntb - 1),
                              tile_position=((0, p0) if p0 else None))
            return ii

        sc.add("tensor", f, waits=w, incs=[(cPE, 1)])
        sc.add("scalar", (lambda g2, p02, c2: lambda s: s.copy(
            ZT[p02:p02 + 64, c2, g2 * 512:(g2 + 1) * 512],
            ps_zt[p02:p02 + 64, :]))(g, p0, c),
            waits=[(cPE, cPE.n)], incs=[(cSA, 1)])
        zt_ev_prev[0] = cSA.n

    K_B2 = int(os.environ.get("K_B2", str(NG * NH)))
    K_TR = int(os.environ.get("K_TR", "1"))
    K_AV = int(os.environ.get("K_AV", "1"))
    cv_jobs = []
    for e_ in range(EX):
        for dc_ in range(NDC):
            for fh_ in range(4):
                cv_jobs.append((w1_h[e_, dc_ * 128:(dc_ + 1) * 128,
                                     fh_ * 1024:(fh_ + 1) * 1024],
                                w1b_h[e_, dc_ * 128:(dc_ + 1) * 128,
                                      fh_ * 1024:(fh_ + 1) * 1024]))
        for rc_ in range(FF // 128):
            cv_jobs.append((w2_h[e_, rc_ * 128:(rc_ + 1) * 128, :],
                            w2b_h[e_, rc_ * 128:(rc_ + 1) * 128, :]))
    cv_out_at = {}

    def emit_cv(i):
        src, dst = cv_jobs[i]
        n = src.shape[-1]
        sl = i % 2
        w = [(cCVO[sl], cv_out_at[i - 2])] if i >= 2 else []
        sc.add("gpsimd", (lambda s2, sl2, n2: lambda g: g.dma_start(
            cv_buf[sl2][:, 0:n2], s2))(src, sl, n), waits=w,
            incs=[(cCVI[sl], 16)])
        sc.add("sync", (lambda d2, sl2, n2: lambda s: s.dma_start(
            d2, cv_buf[sl2][:, 0:n2]))(dst, sl, n),
            waits=[(cCVI[sl], cCVI[sl].n)], incs=[(cCVO[sl], 16)])
        cv_out_at[i] = cCVO[sl].n

    for i_ in range(len(cv_jobs)):
        emit_cv(i_)
    cv_final = [(c_, c_.n) for c_ in cCVO]

    prev_av = None
    ghs = [(g, h) for g in range(NG) for h in range(NH)][:K_B2]
    for g, h in ghs:
        for s_ in range(4):
            emit_scores(g, h, s_)
        for s_ in range(4):
            emit_max(g, h, s_)
        if prev_av is not None and K_AV:
            emit_av(*prev_av)
        for s_ in range(4):
            emit_exp(g, h, s_)
        for s_ in range(4):
            emit_norm(g, h, s_)
        if K_TR:
            for s_ in range(4):
                emit_transposes(g, h, s_)
        prev_av = (g, h) if K_TR else None
    if prev_av is not None and K_AV:
        emit_av(*prev_av)
    sc.run(nc)
    ps2.close()

    if K_PHASES < 3:
        _emit_dbg()
        ctx.close()
        return nc
    ps3 = ExitStack()
    ps_o = ps3.enter_context(nc.psum_tensor("ps_o", [128, 2, 1024], F32))

    sc = Sched(_prev_inc)

    wo_cast_at = []
    for c in range(NHC):
        w = [(cVE, wo_cast_at[c - 4])] if c >= 4 else []
        sc.add("sync", (lambda c2: lambda s: s.dma_start(
            wstage[:, c2 % 4, :], wo_p[c2 * 128:(c2 + 1) * 128, :]))(c),
            waits=w, incs=[(cWS[c % 4], 16)])
        sc.add("vector", (lambda c2: lambda v: v.tensor_copy(
            Wo_bf[:, c2, :], wstage[:, c2 % 4, :]))(c),
            waits=[(cWS[c % 4], cWS[c % 4].n)], incs=[(cVE, 1)])
        wo_cast_at.append(cVE.n)

    def ln_seq(src_par, dst_ap, tc_norm_prev):
        w = [(cSA, tc_norm_prev)] if tc_norm_prev is not None else []
        sc.add("vector", (lambda p2: lambda v: v.reduce_sum(
            lsum[:, :], s1buf[:, p2, :], AX.X))(src_par), waits=w, incs=[(cVE, 1)])
        sc.add("vector", (lambda p2: lambda v: v.scalar_tensor_tensor(
            s1buf[:, 1 - p2, :], s1buf[:, p2, :], 1.0, s1buf[:, p2, :],
            ALU.mult, ALU.mult, accum_out=lsq[:, :]))(src_par), incs=[(cVE, 1)])
        sc.add("vector", lambda v: v.tensor_scalar_mul(lmean[:, :], lsum[:, :],
                                                       1.0 / Dm), incs=[(cVE, 1)])
        sc.add("vector", lambda v: v.tensor_scalar_mul(lvar[:, :], lsq[:, :],
                                                       1.0 / Dm), incs=[(cVE, 1)])
        sc.add("vector", lambda v: v.tensor_mul(lm2[:, :], lmean[:, :], lmean[:, :]),
               incs=[(cVE, 1)])
        sc.add("vector", lambda v: v.tensor_sub(lvar[:, :], lvar[:, :], lm2[:, :]),
               incs=[(cVE, 1)])
        stats_done = cVE.n
        sc.add("scalar", lambda s: s.activation(llog[:, :], lvar[:, :], AF.Ln,
                                                bias=epsb[:, 0:1], scale=1.0),
               waits=[(cVE, stats_done)], incs=[(cSA, 1)])
        sc.add("scalar", lambda s: s.activation(lrstd[:, :], llog[:, :], AF.Exp,
                                                bias=0.0, scale=-0.5),
               incs=[(cSA, 1)])
        rstd_done = cSA.n
        sc.add("vector", lambda v: v.scalar_tensor_tensor(
            lnmb[:, :], lmean[:, :], -1.0, lrstd[:, :], ALU.mult, ALU.mult),
            waits=[(cSA, rstd_done)], incs=[(cVE, 1)])
        nmb_done = cVE.n
        sc.add("scalar", (lambda p2, d2: lambda s: s.activation(
            d2, s1buf[:, p2, :], AF.Identity, bias=lnmb[:, 0:1],
            scale=lrstd[:, 0:1]))(src_par, dst_ap),
            waits=[(cVE, nmb_done)], incs=[(cSA, 1)])
        return cSA.n

    xr_at = {}

    def emit_xr(tc, w):
        sc.add("sync", (lambda t2: lambda s: s.dma_start(
            xstage[:, t2 % 2, :], x_t[t2]))(tc), waits=w,
            incs=[(cXS[tc % 2], 16)])
        xr_at[tc] = cXS[tc % 2].n

    emit_xr(0, [])
    emit_xr(1, [])
    s1_tt_at = {}
    norm_prev = None
    for tc in range(NTC):
        par = tc % 2
        w = [(cVE, wo_cast_at[-1])]
        if tc >= 2:
            w.append((cVE, s1_tt_at[tc - 2]))

        def mk_mm(t2, p2):
            def f(t):
                ii = None
                for half in range(2):
                    sl = ps_o[:, p2, half * 512:(half + 1) * 512]
                    for c in range(NHC):
                        ii = t.matmul(sl, ZT[:, c, t2 * 128:(t2 + 1) * 128],
                                      Wo_bf[:, c, half * 512:(half + 1) * 512],
                                      start=(c == 0), stop=False)
                    ii = t.matmul(sl, ones_row[:, :],
                                  bo_row[:, half * 512:(half + 1) * 512],
                                  start=False, stop=True)
                return ii
            return f

        sc.add("tensor", mk_mm(tc, par), waits=w, incs=[(cPE, 1)])
        sc.add("vector", (lambda t2, p2: lambda v: v.tensor_add(
            s1buf[:, t2 % 2, :], ps_o[:, p2, :], xstage[:, t2 % 2, :]))(tc, par),
            waits=[(cPE, cPE.n), (cXS[tc % 2], xr_at[tc])], incs=[(cVE, 1)])
        s1_tt_at[tc] = cVE.n
        if tc + 2 < NTC:
            emit_xr(tc + 2, [(cVE, s1_tt_at[tc])])
        norm_prev = ln_seq(par, x1[:, tc, :], norm_prev)
    sc.run(nc)
    ps3.close()

    if K_PHASES < 4:
        _emit_dbg()
        ctx.close()
        return nc
    out_at = {}
    w1fh_dma = {}
    htmm_fh = {}

    for tt in range(NTT):
        ps4a = ExitStack()
        ps_x1a = ps4a.enter_context(nc.psum_tensor(f"ps_x1a{tt}", [128, 512], BF16))
        ps_x1b = ps4a.enter_context(nc.psum_tensor(f"ps_x1b{tt}", [128, 512], BF16))
        ps_r = ps4a.enter_context(nc.psum_tensor(f"ps_r{tt}", [128, 8], F32))
        ps_mt = ps4a.enter_context(nc.psum_tensor(f"ps_mt{tt}", [8, 128], F32))
        ps_b2 = ps4a.enter_context(nc.psum_tensor(f"ps_b2{tt}", [128, 1024], F32))

        sc = Sched(_prev_inc)

        def w1q_stream(fhg, waits=()):
            r = fhg % 4
            e2, hh = divmod(fhg, 8)
            f0 = hh * 512
            sc.add("sync", (lambda e3, f3, r3: lambda s: s.dma_start(
                W1R[:, r3, :, :], w1b_p[e3, :, :, f3:f3 + 512]))(e2, f0, r),
                waits=list(waits), incs=[(cWF[r], 16)])
            w1fh_dma[(tt, fhg)] = cWF[r].n

        for fhg in (0, 1):
            w = list(cv_final) if tt == 0 and fhg == 0 else []
            w1q_stream(fhg, w)

        xt1_evac = {}
        rtr_done = {}
        mprev_expdone = None
        mt_prev_ev = None
        b2_prev_ev = None
        for tc4 in range(4):
            tc = tt * 4 + tc4
            w = []
            if tc4 >= 2:
                w.append((cSA, xt1_evac[(tc4 - 2, 1)]))
            sc.add("vector", (lambda t2, q2: lambda v: v.tensor_copy(
                xbstage[:, q2 % 2, :], x1[:, t2, :]))(tc, tc4),
                waits=w, incs=[(cVE, 1)])
            cast_mark = cVE.n
            for pack in range(2):
                ps = ps_x1a if pack == 0 else ps_x1b
                w = [(cVE, cast_mark)]
                if tc4 > 0:
                    w.append((cSA, xt1_evac[(tc4 - 1, pack)]))

                def mk_tr(q2, p2, ps2):
                    def f(t):
                        ii = None
                        for j in range(4):
                            dc = p2 * 4 + j
                            ii = t.transpose(
                                ps2[:, j * 128:(j + 1) * 128],
                                xbstage[:, q2 % 2, dc * 128:(dc + 1) * 128],
                                identb[:, :])
                        return ii
                    return f

                sc.add("tensor", mk_tr(tc4, pack, ps), waits=w, incs=[(cPE, 1)])

                def mk_ev(q2, p2, ps2):
                    def f(s):
                        return s.copy(
                            XT1[:, p2 * 4:(p2 + 1) * 4, q2 * 128:(q2 + 1) * 128],
                            ps2[:, :].rearrange("p (a b) -> p a b", a=4))
                    return f

                sc.add("scalar", mk_ev(tc4, pack, ps), waits=[(cPE, cPE.n)],
                       incs=[(cSA, 1)])
                xt1_evac[(tc4, pack)] = cSA.n

            w = [(cSA, xt1_evac[(tc4, 1)])]
            if mprev_expdone is not None:
                w.append((cSA, mprev_expdone))

            def mk_rmm(q2):
                def f(t):
                    ii = None
                    for dc in range(NDC):
                        ii = t.matmul(ps_r[:, :],
                                      XT1[:, dc, q2 * 128:(q2 + 1) * 128],
                                      wr_bf[:, dc, :],
                                      start=(dc == 0), stop=(dc == NDC - 1))
                    return ii
                return f

            sc.add("tensor", mk_rmm(tc4), waits=w, incs=[(cPE, 1)])
            sc.add("vector", lambda v: v.reduce_max(smax[:, :], ps_r[:, :], AX.X,
                                                    negate=True),
                   waits=[(cPE, cPE.n)], incs=[(cVE, 1)])
            sc.add("scalar", lambda s: s.activation(
                probs[:, :], ps_r[:, :], AF.Exp, bias=smax[:, 0:1], scale=1.0,
                accum_out=ssum[:, :]), waits=[(cVE, cVE.n)], incs=[(cSA, 1)])
            mprev_expdone = cSA.n
            sc.add("vector", lambda v: v.reciprocal(srr[:, :], ssum[:, :]),
                   waits=[(cSA, cSA.n)], incs=[(cVE, 1)])
            sc.add("vector", lambda v: v.tensor_scalar_mul(probs[:, :], probs[:, :],
                                                           srr[:, 0:1]),
                   incs=[(cVE, 1)])
            sc.add("vector", lambda v: v.reduce_max(pmax[:, :], probs[:, :], AX.X),
                   incs=[(cVE, 1)])
            sc.add("vector", (lambda q2: lambda v: v.scalar_tensor_tensor(
                Mw[:, q2, :], probs[:, :], pmax[:, 0:1], probs[:, :],
                ALU.is_equal, ALU.mult))(tc4), incs=[(cVE, 1)])
            m_done = cVE.n

            w = [(cVE, m_done)]
            if mt_prev_ev is not None:
                w.append((cSA, mt_prev_ev))
            sc.add("tensor", (lambda q2: lambda t: t.transpose(
                ps_mt[:, :], Mw[:, q2, :], identf[:, :]))(tc4),
                waits=w, incs=[(cPE, 1)])
            sc.add("scalar", (lambda q2: lambda s: s.copy(
                MT_bf[0:8, q2 * 128:(q2 + 1) * 128], ps_mt[:, :]))(tc4),
                waits=[(cPE, cPE.n)], incs=[(cSA, 1)])
            mt_prev_ev = cSA.n

            w = [(cSA, mt_prev_ev)]
            if b2_prev_ev is not None:
                w.append((cVE, b2_prev_ev))
            if tt > 0:
                w.append((cOUTs[tc4], out_at[(tt - 1, tc4)]))

            def mk_b2(q2):
                def f(t):
                    ii = None
                    for half in range(2):
                        ii = t.matmul(ps_b2[:, half * 512:(half + 1) * 512],
                                      MT_bf[0:8, q2 * 128:(q2 + 1) * 128],
                                      b2s[0:8, half * 512:(half + 1) * 512],
                                      start=True, stop=True)
                    return ii
                return f

            sc.add("tensor", mk_b2(tc4), waits=w, incs=[(cPE, 1)])
            sc.add("vector", (lambda q2: lambda v: v.tensor_copy(
                acc[:, q2, :], ps_b2[:, :]))(tc4), waits=[(cPE, cPE.n)],
                incs=[(cVE, 1)])
            b2_prev_ev = cVE.n
        sc.run(nc)
        ps4a.close()

        ps4b = ExitStack()
        ps_all = ps4b.enter_context(nc.psum_tensor(f"ps_all{tt}", [128, 4096], F32))
        sc = Sched(_prev_inc)

        ht_ev_at = {}
        y_ev_at = {}
        w2dma_at = {}
        ymm_at = {}
        w2_ymm_hist = {}

        def w1q_stream(fhg):
            r = fhg % 4
            e2, hh = divmod(fhg, 8)
            f0 = hh * 512
            w = []
            if fhg - 4 >= 0:
                w.append((cPE, htmm_fh[(tt, fhg - 4)]))
            sc.add("sync", (lambda e3, f3, r3: lambda s: s.dma_start(
                W1R[:, r3, :, :], w1b_p[e3, :, :, f3:f3 + 512]))(e2, f0, r),
                waits=w, incs=[(cWF[r], 16)])
            w1fh_dma[(tt, fhg)] = cWF[r].n

        W2SLOT_BF = [W2bf[:, 0, :], W2bf[:, 1, :], W2bf2[:, 0, :], W2bf2[:, 1, :]]

        def w2_dma(e, fc, extra=()):
            r = fc % 4
            w = list(extra)
            key = (e, fc - 4) if fc >= 4 else ((e - 1, 28 + fc) if e > 0 else None)
            if key is not None and key in w2_ymm_hist:
                w.append((cPE, w2_ymm_hist[key]))
            sc.add("scalar", (lambda e2, f2, r2: lambda s: s.dma_start(
                W2SLOT_BF[r2],
                w2b_h[e2, f2 * 128:(f2 + 1) * 128, :]))(e, fc, r),
                waits=w, incs=[(cW2R[r], 16)])
            w2dma_at[(e, fc)] = cW2R[r].n

        for j in range(3):
            w2_dma(0, j, list(cv_final) if (tt == 0 and j == 0) else ())

        for e in range(EX):
            for fc in range(NFC):
                fhg = e * 8 + fc // 4
                if fc % 4 == 0:
                    if fhg + 2 <= EX * 8 - 1:
                        w1q_stream(fhg + 2)
                bank = fc % 4
                r_ = (e * 8 + fc // 4) % 4
                w = [(cWF[r_], w1fh_dma[(tt, fhg)])]
                if (e, fc - 4) in ht_ev_at:
                    w.append((cSA, ht_ev_at[(e, fc - 4)]))
                elif fc < 4 and e > 0:
                    w.append((cSA, ht_ev_at[(e - 1, 28 + fc)]))
                if fc < 4 and e > 0:
                    w.append((cVE, y_ev_at[(e - 1, 0 if fc < 2 else 1)]))

                def mk_ht(e2, f2, g2, b2_):
                    def f(t):
                        ii = None
                        for dc in range(NDC):
                            ii = t.matmul(
                                ps_all[:, b2_ * 512:(b2_ + 1) * 512],
                                W1R[:, g2 % 4, dc, (f2 % 4) * 128:(f2 % 4 + 1) * 128],
                                XT1[:, dc, :],
                                start=(dc == 0), stop=(dc == NDC - 1))
                        return ii
                    return f

                sc.add("tensor", mk_ht(e, fc, fhg, bank), waits=w, incs=[(cPE, 1)])
                htmm_fh[(tt, fhg)] = cPE.n

                def mk_htev(e2, f2, b2_):
                    def f(s):
                        return s.activation(
                            HT[:, f2, :], ps_all[:, b2_ * 512:(b2_ + 1) * 512],
                            AF.Relu, bias=b1s[:, e2, f2:f2 + 1], scale=1.0)
                    return f

                sc.add("scalar", mk_htev(e, fc, bank), waits=[(cPE, cPE.n)],
                       incs=[(cSA, 1)])
                ht_ev_at[(e, fc)] = cSA.n

            for fc in range(NFC):
                w = [(cW2R[fc % 4], w2dma_at[(e, fc)])]
                if fc == 0:
                    w.append((cSA, ht_ev_at[(e, NFC - 1)]))
                    for q in range(1, 4):
                        if e > 0:
                            w.append((cVE, y_ev_at[(e - 1, q)]))

                def mk_y(e2, f2):
                    def f(t):
                        ii = None
                        rhs = W2SLOT_BF[f2 % 4]
                        for q in range(4):
                            for half in range(2):
                                ii = t.matmul(
                                    ps_all[:, q * 1024 + half * 512:
                                           q * 1024 + (half + 1) * 512],
                                    HT[:, f2, q * 128:(q + 1) * 128],
                                    rhs[:, half * 512:(half + 1) * 512],
                                    start=(f2 == 0), stop=(f2 == NFC - 1))
                        return ii
                    return f

                sc.add("tensor", mk_y(e, fc), waits=w, incs=[(cPE, 1)])
                w2_ymm_hist[(e, fc)] = cPE.n
                nd = (e, fc + 3) if fc + 3 < NFC else (e + 1, fc + 3 - NFC)
                if nd[0] < EX and nd not in w2dma_at:
                    w2_dma(*nd)
            ymm_at[e] = cPE.n

            for q in range(4):
                sc.add("vector", (lambda e2, q2: lambda v: v.scalar_tensor_tensor(
                    acc[:, q2, :], ps_all[:, q2 * 1024:(q2 + 1) * 1024],
                    Mw[:, q2, e2:e2 + 1], acc[:, q2, :],
                    ALU.mult, ALU.add))(e, q),
                    waits=[(cPE, ymm_at[e])], incs=[(cVE, 1)])
                y_ev_at[(e, q)] = cVE.n

        norm_prev2 = None
        for q in range(4):
            tc = tt * 4 + q
            sc.add("vector", (lambda t2, q2: lambda v: v.tensor_add(
                s1buf[:, q2 % 2, :], acc[:, q2, :], x1[:, t2, :]))(tc, q),
                waits=[(cVE, y_ev_at[(EX - 1, q)])], incs=[(cVE, 1)])

            w = [(cSA, norm_prev2)] if norm_prev2 is not None else []
            sc.add("vector", (lambda q2: lambda v: v.reduce_sum(
                lsum[:, :], s1buf[:, q2 % 2, :], AX.X))(q), waits=w, incs=[(cVE, 1)])
            sc.add("vector", (lambda q2: lambda v: v.scalar_tensor_tensor(
                s1buf[:, 1 - q2 % 2, :], s1buf[:, q2 % 2, :], 1.0,
                s1buf[:, q2 % 2, :], ALU.mult, ALU.mult,
                accum_out=lsq[:, :]))(q), incs=[(cVE, 1)])
            sc.add("vector", lambda v: v.tensor_scalar_mul(lmean[:, :], lsum[:, :],
                                                           1.0 / Dm), incs=[(cVE, 1)])
            sc.add("vector", lambda v: v.tensor_scalar_mul(lvar[:, :], lsq[:, :],
                                                           1.0 / Dm), incs=[(cVE, 1)])
            sc.add("vector", lambda v: v.tensor_mul(lm2[:, :], lmean[:, :],
                                                    lmean[:, :]), incs=[(cVE, 1)])
            sc.add("vector", lambda v: v.tensor_sub(lvar[:, :], lvar[:, :],
                                                    lm2[:, :]), incs=[(cVE, 1)])
            stats_done = cVE.n
            sc.add("scalar", lambda s: s.activation(llog[:, :], lvar[:, :], AF.Ln,
                                                    bias=epsb[:, 0:1], scale=1.0),
                   waits=[(cVE, stats_done)], incs=[(cSA, 1)])
            sc.add("scalar", lambda s: s.activation(lrstd[:, :], llog[:, :], AF.Exp,
                                                    bias=0.0, scale=-0.5),
                   incs=[(cSA, 1)])
            sc.add("vector", lambda v: v.scalar_tensor_tensor(
                lnmb[:, :], lmean[:, :], -1.0, lrstd[:, :], ALU.mult, ALU.mult),
                waits=[(cSA, cSA.n)], incs=[(cVE, 1)])
            sc.add("scalar", (lambda q2: lambda s: s.activation(
                acc[:, q2, :], s1buf[:, q2 % 2, :], AF.Identity, bias=lnmb[:, 0:1],
                scale=lrstd[:, 0:1]))(q), waits=[(cVE, cVE.n)], incs=[(cSA, 1)])
            norm_prev2 = cSA.n
            sc.add("sync", (lambda t2, q2: lambda s: s.dma_start(
                out_t[t2], acc[:, q2, :]))(tc, q), waits=[(cSA, norm_prev2)],
                incs=[(cOUTs[q], 16)])
            out_at[(tt, q)] = cOUTs[q].n

        if tt == NTT - 1:
            sc.add("gpsimd", lambda g: g.memset(epsb[:, :], 1e-5),
                   waits=[(c_, c_.n) for c_ in cOUTs])
        sc.run(nc)
        ps4b.close()

    _emit_dbg()
    ctx.close()
    return nc


_NC_CACHE = None


def kernel(**inputs):
    global _NC_CACHE
    order = ["L0_attn_residual", "L0_attn_Wq", "L0_attn_Wk", "L0_attn_Wv",
             "L0_attn_Wo", "L0_attn_bq", "L0_attn_bk", "L0_attn_bv",
             "L0_attn_bo", "L0_ffn_w_router", "L0_ffn_w1", "L0_ffn_b1",
             "L0_ffn_w2", "L0_ffn_b2"]
    a = {k: np.ascontiguousarray(np.asarray(inputs[k], dtype=np.float32))
         for k in order}
    if _NC_CACHE is None:
        _NC_CACHE = build_nc()
    nc = _NC_CACHE
    shared = {
        "wq": a["L0_attn_Wq"], "wk": a["L0_attn_Wk"], "wv": a["L0_attn_Wv"],
        "wo": a["L0_attn_Wo"], "bq": a["L0_attn_bq"], "bk": a["L0_attn_bk"],
        "bv": a["L0_attn_bv"], "bo": a["L0_attn_bo"].reshape(1, Dm),
        "wr": a["L0_ffn_w_router"], "w1": a["L0_ffn_w1"], "b1": a["L0_ffn_b1"],
        "w2": a["L0_ffn_w2"], "b2": a["L0_ffn_b2"],
    }
    x = a["L0_attn_residual"]
    in_maps = [dict(shared, x=np.ascontiguousarray(x[c])) for c in range(8)]
    res = run_bass_kernel_spmd(nc, in_maps, core_ids=list(range(8)))
    return np.stack([np.asarray(res.results[c]["out"]) for c in range(8)], axis=0)


if __name__ == "__main__":
    nc = build_nc()
    print("built ok, instructions:", nc.n_instructions())


# version 11
# speedup vs baseline: 1.1432x; 1.1432x over previous
import os
import numpy as np
from contextlib import ExitStack

import concourse.bass as bass
import concourse.mybir as mybir
from concourse.bass_utils import run_bass_kernel_spmd

F32 = mybir.dt.float32
BF16 = mybir.dt.bfloat16
AF = mybir.ActivationFunctionType
ALU = mybir.AluOpType
AX = mybir.AxisListType

S, Dm, NH, HA, FF, EX = 2048, 1024, 16, 64, 4096, 8
NSC, NDC, NHC, NTC, NFC, NFB, NTT, NG = 16, 8, 8, 16, 32, 4, 4, 4

ENGN = ("tensor", "vector", "scalar", "sync", "gpsimd")


class Ctr:
    __slots__ = ("sem", "n")

    def __init__(self, nc, name):
        self.sem = nc.alloc_semaphore(name=name)
        self.n = 0


class Sched:
    def __init__(self, prev_inc=None):
        self.ops = {e: [] for e in ENGN}
        self.prev_inc = prev_inc if prev_inc is not None else {}

    def add(self, eng, emit, waits=(), incs=()):
        waits = list(waits)
        pi = self.prev_inc.get(eng)
        if pi is not None:
            ctr0, v0 = pi
            if not any(c is ctr0 and v >= v0 for c, v in waits):
                waits.insert(0, (ctr0, v0))
        rincs = []
        for ctr, k in incs:
            ctr.n += k
            rincs.append((ctr, k))
        if rincs:
            self.prev_inc[eng] = (rincs[-1][0], rincs[-1][0].n)
        self.ops[eng].append((tuple(waits), emit, rincs))

    def run(self, nc):
        with nc.Block() as blk:
            for name in ENGN:
                ops = self.ops[name]
                if not ops:
                    continue

                def make(ops_):
                    def _f(eng):
                        for waits, emit, incs in ops_:
                            for ctr, v in waits:
                                eng.wait_ge(ctr.sem, v)
                            inst = emit(eng)
                            for ctr, k in incs:
                                inst.then_inc(ctr.sem, k)
                    return _f

                getattr(blk, name)(make(ops))


K_PHASES = int(os.environ.get("K_PHASES", "9"))


def build_nc():
    nc = bass.Bass("TRN2", target_bir_lowering=False)

    x_h = nc.dram_tensor("x", [S, Dm], F32, kind="ExternalInput")
    wq_h = nc.dram_tensor("wq", [NH, Dm, HA], F32, kind="ExternalInput")
    wk_h = nc.dram_tensor("wk", [NH, Dm, HA], F32, kind="ExternalInput")
    wv_h = nc.dram_tensor("wv", [NH, Dm, HA], F32, kind="ExternalInput")
    wo_h = nc.dram_tensor("wo", [NH, HA, Dm], F32, kind="ExternalInput")
    bq_h = nc.dram_tensor("bq", [NH, HA], F32, kind="ExternalInput")
    bk_h = nc.dram_tensor("bk", [NH, HA], F32, kind="ExternalInput")
    bv_h = nc.dram_tensor("bv", [NH, HA], F32, kind="ExternalInput")
    bo_h = nc.dram_tensor("bo", [1, Dm], F32, kind="ExternalInput")
    wr_h = nc.dram_tensor("wr", [Dm, EX], F32, kind="ExternalInput")
    w1_h = nc.dram_tensor("w1", [EX, Dm, FF], F32, kind="ExternalInput")
    b1_h = nc.dram_tensor("b1", [EX, FF], F32, kind="ExternalInput")
    w2_h = nc.dram_tensor("w2", [EX, FF, Dm], F32, kind="ExternalInput")
    b2_h = nc.dram_tensor("b2", [EX, Dm], F32, kind="ExternalInput")
    out_h = nc.dram_tensor("out", [S, Dm], F32, kind="ExternalOutput")
    w1b_h = nc.dram_tensor("w1b", [EX, Dm, FF], BF16, kind="Internal")
    w2b_h = nc.dram_tensor("w2b", [EX, FF, Dm], BF16, kind="Internal")

    ctx = ExitStack()

    AR_BYTES = 204288
    ar = ctx.enter_context(nc.sbuf_tensor("arena", [128, AR_BYTES], mybir.dt.uint8))

    def V(off, dtype, *dims):
        n = int(np.prod(dims)) * mybir.dt.size(dtype)
        assert off % 64 == 0 and off + n <= AR_BYTES, (off, n)
        ap = ar[:, off:off + n].bitcast(dtype)
        if len(dims) > 1:
            names = " ".join(f"d{i}" for i in range(len(dims)))
            kw = {f"d{i}": dims[i] for i in range(len(dims) - 1)}
            ap = ap.rearrange(f"p ({names}) -> p {names}", **kw)
        return ap

    O_IDB, O_IDF, O_MADD = 0, 256, 768
    O_BQS, O_BKS, O_B1S, O_WRB = 1280, 1344, 1408, 2432
    O_PROBS, O_M, O_SM = 2560, 2624, 2752
    O_STX, O_STW, O_STW2 = 4800, 12992, 21184
    O_STXB, O_S1 = 29376, 33472
    D0 = 41664
    O_QT, O_KT, O_VT = D0, D0 + 32768, D0 + 65536
    O_XT0, O_WSL = D0 + 98304, D0 + 131072
    O_PT, O_ZT = D0 + 98304, D0 + 114688
    O_PBF, O_PSB = O_STX, O_STW2
    O_WO, O_X1 = D0 + 98304, D0
    O_XT1, O_HT, O_ACC, O_W1B = D0 + 65536, D0 + 73728, D0 + 106496, D0 + 122880
    O_W2B = O_STXB
    assert D0 + 155648 <= AR_BYTES

    identb = V(O_IDB, BF16, 128)
    identf = V(O_IDF, F32, 128)
    madd = V(O_MADD, F32, 128)
    bq_s = V(O_BQS, F32, 8)
    bk_s = V(O_BKS, F32, 8)
    b1s = V(O_B1S, F32, EX, 32)
    wr_bf = V(O_WRB, BF16, 8, 8)
    probs = V(O_PROBS, F32, 8)
    Mw = V(O_M, F32, 4, 8)

    def SM(i):
        return V(O_SM + 64 * i, F32, 1)

    smax, ssum, srr, pmax = SM(0), SM(1), SM(2), SM(3)
    lsum, lsq, lmean, lvar = SM(4), SM(5), SM(6), SM(7)
    lrstd, lnmb, epsb, lm2, llog = SM(8), SM(9), SM(10), SM(11), SM(12)

    xstage = V(O_STX, F32, 2, 1024)
    wstage = V(O_STW, F32, 4, 1024)
    w2stage = V(O_STX, F32, 2, 1024)
    xbstage = V(O_STXB, BF16, 2, 1024)
    s1buf = V(O_S1, F32, 2, 1024)

    QT = V(O_QT, BF16, 8, 2048)
    KT = V(O_KT, BF16, 8, 2048)
    Vtm = V(O_VT, BF16, 16, 1024)
    XT0 = V(O_XT0, BF16, 8, 2048)
    Wslot = V(O_WSL, BF16, 8, 1024)
    PTg = V(O_PT, BF16, 16, 512)
    ZT = V(O_ZT, BF16, 8, 2048)
    P_bf = V(O_PBF, BF16, 2048)
    P_sb = V(O_PSB, F32, 2048)
    Wo_bf = V(O_WO, BF16, 8, 1024)
    x1 = V(O_X1, F32, 16, 1024)
    XT1 = V(O_XT1, BF16, 8, 512)
    HT = V(O_HT, BF16, 32, 512)
    acc = V(O_ACC, F32, 4, 1024)
    W1bf = V(O_W1B, BF16, 2, 8, 1024)
    W1R = V(O_W1B, BF16, 4, 8, 512)
    whf = V(O_STW, F32, 8, 512)
    W2bf = V(O_W2B, BF16, 2, 1024)
    w2b2 = V(O_S1, F32, 2, 1024)
    W2bf2 = V(D0 + 155648, BF16, 2, 1024)
    cv_buf = [V(O_STW, BF16, 2048), V(O_STW + 4096, BF16, 2048),
              V(O_S1, BF16, 2048), V(O_S1 + 4096, BF16, 2048)]

    ones_row = ctx.enter_context(nc.sbuf_tensor("ones_row", [1, 128], BF16))
    bv_row = ctx.enter_context(nc.sbuf_tensor("bv_row", [1, Dm], BF16))
    bo_row = ctx.enter_context(nc.sbuf_tensor("bo_row", [1, Dm], BF16))
    b2s = ctx.enter_context(nc.sbuf_tensor("b2s", [EX, Dm], BF16))
    MT_bf = ctx.enter_context(nc.sbuf_tensor("MT_bf", [EX, 512], BF16))
    bstage = ctx.enter_context(nc.sbuf_tensor("bstage", [32, 128], F32))

    cXS = [Ctr(nc, "cXS0"), Ctr(nc, "cXS1")]
    cWS = [Ctr(nc, "cWS0"), Ctr(nc, "cWS1"), Ctr(nc, "cWS2"), Ctr(nc, "cWS3")]
    cWF = [Ctr(nc, f"cWF{i}") for i in range(8)]
    cW2R = [Ctr(nc, f"cW2R{i}") for i in range(4)]
    cCVI = [Ctr(nc, f"cCVI{i}") for i in range(4)]
    cCVO = [Ctr(nc, f"cCVO{i}") for i in range(4)]
    cW2S = [Ctr(nc, "cW2S0"), Ctr(nc, "cW2S1")]
    cBS = Ctr(nc, "cBS")
    cOUTs = [Ctr(nc, "cOUT0"), Ctr(nc, "cOUT1"), Ctr(nc, "cOUT2"), Ctr(nc, "cOUT3")]
    cGP = Ctr(nc, "cGP")
    cPE = Ctr(nc, "cPE")
    cVE = Ctr(nc, "cVE")
    cSA = Ctr(nc, "cSA")
    cGC = Ctr(nc, "cGC")
    cOUT = Ctr(nc, "cOUT")


    def _emit_dbg():
        dbg_name = os.environ.get("K_DBG", "")
        if not dbg_name:
            return
        dbg_h = nc.dram_tensor("dbg", [128, 16384], F32, kind="ExternalOutput")
        region = {
            "x1": x1[:, :, :].rearrange("p a b -> p (a b)"),
            "psb": P_sb[:, :],
            "acc": acc[:, :, :].rearrange("p a b -> p (a b)"),
            "qt": QT[:, :, :].rearrange("p a b -> p (a b)"),
            "kt": KT[:, :, :].rearrange("p a b -> p (a b)"),
            "vt": Vtm[:, :, :].rearrange("p a b -> p (a b)"),
            "zt": ZT[:, :, :].rearrange("p a b -> p (a b)"),
            "xt0": XT0[:, :, :].rearrange("p a b -> p (a b)"),
            "xt1": XT1[:, :, :].rearrange("p a b -> p (a b)"),
            "ht": HT[:, :, :].rearrange("p a b -> p (a b)"),
            "pbf": P_bf[:, :],
            "ptg": PTg[:, :, :].rearrange("p a b -> p (a b)"),
            "m": Mw[:, :, :].rearrange("p a b -> p (a b)"),
        }[dbg_name]
        with nc.Block() as blk:
            @blk.gpsimd
            def _(g):
                g.dma_start(dbg_h[:, 0:region.free_size()],
                            region).then_inc(cGP.sem, 16)
                g.wait_ge(cGP.sem, cGP.n + 16)
        cGP.n += 16

    w1b_p = w1b_h.ap().rearrange("e (dc p) f -> e p dc f", p=128)
    x_t = x_h.ap().rearrange("(n p) d -> n p d", p=128)
    out_t = out_h.ap().rearrange("(n p) d -> n p d", p=128)
    wq_p = wq_h.ap().rearrange("h d a -> d h a")
    wk_p = wk_h.ap().rearrange("h d a -> d h a")
    wv_p = wv_h.ap().rearrange("h d a -> d h a")
    wo_p = wo_h.ap().rearrange("h a d -> (h a) d")
    wr_p = wr_h.ap().rearrange("(c p) e -> p c e", p=128)
    bq_p = bq_h.ap().rearrange("(g y) a -> g (y a)", g=8)
    bk_p = bk_h.ap().rearrange("(g y) a -> g (y a)", g=8)
    bv_p = bv_h.ap().rearrange("h a -> (h a)").unsqueeze(0)
    b1_p = b1_h.ap().rearrange("e (p c) -> e p c", p=32)

    _prev_inc = {}

    ps0 = ExitStack()
    ps_xa = ps0.enter_context(nc.psum_tensor("ps_xa", [128, 512], BF16))
    ps_xb = ps0.enter_context(nc.psum_tensor("ps_xb", [128, 512], BF16))
    ps_bias = ps0.enter_context(nc.psum_tensor("ps_bias", [128, 32], F32))

    sc = Sched(_prev_inc)

    def g_consts0(g):
        g.memset(identb[:, :], 0.0)
        g.memset(identf[:, :], 0.0)
        g.memset(madd[:, :], 0.0)
        g.memset(ones_row[:, :], 1.0)
        return g.memset(epsb[:, :], 1e-5)

    sc.add("gpsimd", g_consts0, incs=[(cGC, 1)])
    _gc0 = cGC.n

    def g_consts1(g):
        g.affine_select(identb[:, :], identb[:, :], [[-1, 128]], ALU.not_equal,
                        1.0, base=0, channel_multiplier=1)
        g.affine_select(identf[:, :], identf[:, :], [[-1, 128]], ALU.not_equal,
                        1.0, base=0, channel_multiplier=1)
        return g.affine_select(madd[:, :], madd[:, :], [[-1, 128]], ALU.is_ge,
                               -100000.0, base=0, channel_multiplier=1)

    sc.add("gpsimd", g_consts1, waits=[(cGC, _gc0)], incs=[(cGC, 1)])
    sc.add("gpsimd", lambda g: g.dma_start(bv_row[:, :], bv_p), incs=[(cGP, 16)])
    sc.add("gpsimd", lambda g: g.dma_start(bo_row[:, :], bo_h[0:1, :]), incs=[(cGP, 16)])
    sc.add("gpsimd", lambda g: g.dma_start(b2s[:, :], b2_h[:, :]), incs=[(cGP, 16)])

    xdma_at, xcast_at, xtr_evac_at = {}, {}, {}

    def emit_xdma(tc):
        w = [(cVE, xcast_at[tc - 2])] if tc >= 2 else []
        sc.add("sync", (lambda t2: lambda s: s.dma_start(
            xstage[:, t2 % 2, :], x_t[t2]))(tc), waits=w,
            incs=[(cXS[tc % 2], 16)])
        xdma_at[tc] = cXS[tc % 2].n

    emit_xdma(0)
    emit_xdma(1)
    for tc in range(NTC):
        w = [(cXS[tc % 2], xdma_at[tc])]
        if tc >= 2:
            w.append((cSA, xtr_evac_at[(tc - 2, 1)]))
        sc.add("vector", (lambda t2: lambda v: v.tensor_copy(
            xbstage[:, t2 % 2, :], xstage[:, t2 % 2, :]))(tc),
            waits=w, incs=[(cVE, 1)])
        xcast_at[tc] = cVE.n
        if tc + 2 < NTC:
            emit_xdma(tc + 2)
        for pack in range(2):
            ps = ps_xa if pack == 0 else ps_xb
            w = [(cVE, xcast_at[tc]), (cGC, 2)]
            if tc > 0:
                w.append((cSA, xtr_evac_at[(tc - 1, pack)]))

            def mk_tr(t2, p2, ps2):
                def f(t):
                    ii = None
                    for j in range(4):
                        dc = p2 * 4 + j
                        ii = t.transpose(ps2[:, j * 128:(j + 1) * 128],
                                         xbstage[:, t2 % 2, dc * 128:(dc + 1) * 128],
                                         identb[:, :])
                    return ii
                return f

            sc.add("tensor", mk_tr(tc, pack, ps), waits=w, incs=[(cPE, 1)])
            pe_mark = cPE.n

            def mk_ev(t2, p2, ps2):
                def f(s):
                    return s.copy(
                        XT0[:, p2 * 4:(p2 + 1) * 4, t2 * 128:(t2 + 1) * 128],
                        ps2[:, :].rearrange("p (a b) -> p a b", a=4))
                return f

            sc.add("scalar", mk_ev(tc, pack, ps), waits=[(cPE, pe_mark)],
                   incs=[(cSA, 1)])
            xtr_evac_at[(tc, pack)] = cSA.n

    bias_jobs = [
        (bq_p, 8, lambda s: s.activation(bq_s[:, :], ps_bias[:, 0:8], AF.Identity,
                                         bias=0.0, scale=0.125)),
        (bk_p, 8, lambda s: s.activation(bk_s[:, :], ps_bias[:, 0:8], AF.Identity,
                                         bias=0.0, scale=1.0)),
    ]
    for e_ in range(EX):
        bias_jobs.append((b1_p[e_], 32, (lambda e2: lambda s: s.activation(
            b1s[:, e2, :], ps_bias[:, 0:32], AF.Identity, bias=0.0, scale=1.0))(e_)))
    prev_tr, prev_ev = None, None
    for j, (src, rows, evac) in enumerate(bias_jobs):
        w = [(cPE, prev_tr)] if prev_tr is not None else []
        sc.add("sync", (lambda s2, r2: lambda s: s.dma_start(
            bstage[0:r2, :], s2))(src, rows), waits=w, incs=[(cBS, 16)])
        w = [(cBS, cBS.n), (cGC, 2)]
        if prev_ev is not None:
            w.append((cSA, prev_ev))
        sc.add("tensor", (lambda r2: lambda t: t.transpose(
            ps_bias[:, 0:r2], bstage[0:r2, :], identf[0:r2, 0:r2]))(rows),
            waits=w, incs=[(cPE, 1)])
        prev_tr = cPE.n
        sc.add("scalar", evac, waits=[(cPE, cPE.n)], incs=[(cSA, 1)])
        prev_ev = cSA.n

    sc.add("sync", lambda s: s.dma_start(
        wstage[:, 0, 0:64].rearrange("p (c e) -> p c e", c=8), wr_p),
        incs=[(cWS[0], 16)])
    sc.add("vector", lambda v: v.tensor_copy(
        wr_bf[:, :, :].rearrange("p c e -> p (c e)"), wstage[:, 0, 0:64]),
        waits=[(cWS[0], cWS[0].n)], incs=[(cVE, 1)])
    sc.run(nc)
    ps0.close()

    if K_PHASES < 1:
        _emit_dbg()
        ctx.close()
        return nc
    ps1 = ExitStack()
    ps_qk = ps1.enter_context(nc.psum_tensor("ps_qk", [128, 4, 512], F32))
    ps_v = ps1.enter_context(nc.psum_tensor("ps_v", [128, 1024], F32))

    sc = Sched(_prev_inc)
    wl_pe_done = []
    bank_last_evac = {}

    def wload(widx, wsrc):
        cast_at = []
        for dc in range(NDC):
            w = []
            if dc >= 4:
                w.append((cVE, cast_at[dc - 4]))
            if dc < 4 and widx > 0:
                w.append((cPE, wl_pe_done[widx - 1]))
            sc.add("sync", (lambda w2, d2: lambda s: s.dma_start(
                wstage[:, d2 % 4, :].rearrange("p (h a) -> p h a", h=16),
                w2[d2 * 128:(d2 + 1) * 128]))(wsrc, dc),
                waits=w, incs=[(cWS[dc % 4], 16)])
            w2_ = [(cWS[dc % 4], cWS[dc % 4].n)]
            if widx > 0:
                w2_.append((cPE, wl_pe_done[widx - 1]))
            sc.add("vector", (lambda d2: lambda v: v.tensor_copy(
                Wslot[:, d2, :], wstage[:, d2 % 4, :]))(dc),
                waits=w2_, incs=[(cVE, 1)])
            cast_at.append(cVE.n)
        return cast_at

    for widx, (wsrc, qt, scale, bias_ap) in enumerate(
            [(wq_p, QT, 0.125, bq_s), (wk_p, KT, 1.0, bk_s)]):
        cast_at = wload(widx, wsrc)
        for c in range(NHC):
            for st in range(4):
                w = [(cVE, cast_at[-1])]
                if st in bank_last_evac:
                    w.append((cSA, bank_last_evac[st]))

                def mk_mm(c2, st2):
                    def f(t):
                        ii = None
                        for dc in range(NDC):
                            ii = t.matmul(ps_qk[:, st2, :],
                                          Wslot[:, dc, c2 * 128:(c2 + 1) * 128],
                                          XT0[:, dc, st2 * 512:(st2 + 1) * 512],
                                          start=(dc == 0), stop=(dc == NDC - 1))
                        return ii
                    return f

                sc.add("tensor", mk_mm(c, st), waits=w, incs=[(cPE, 1)])

                def mk_ev(c2, st2, qt2, sc2, b2_):
                    def f(s):
                        return s.activation(
                            qt2[:, c2, st2 * 512:(st2 + 1) * 512],
                            ps_qk[:, st2, :], AF.Identity,
                            bias=b2_[:, c2:c2 + 1], scale=sc2)
                    return f

                sc.add("scalar", mk_ev(c, st, qt, scale, bias_ap),
                       waits=[(cPE, cPE.n)], incs=[(cSA, 1)])
                bank_last_evac[st] = cSA.n
        wl_pe_done.append(cPE.n)

    cast_at = wload(2, wv_p)
    v_evac_prev = None
    for tc in range(NTC):
        w = [(cVE, cast_at[-1])]
        if tc == 0:
            w.append((cGP, 48))
        if v_evac_prev is not None:
            w.append((cVE, v_evac_prev))

        def mk_vmm(t2):
            def f(t):
                ii = None
                for half in range(2):
                    sl = ps_v[:, half * 512:(half + 1) * 512]
                    for dc in range(NDC):
                        ii = t.matmul(sl, XT0[:, dc, t2 * 128:(t2 + 1) * 128],
                                      Wslot[:, dc, half * 512:(half + 1) * 512],
                                      start=(dc == 0), stop=False)
                    ii = t.matmul(sl, ones_row[:, :],
                                  bv_row[:, half * 512:(half + 1) * 512],
                                  start=False, stop=True)
                return ii
            return f

        sc.add("tensor", mk_vmm(tc), waits=w, incs=[(cPE, 1)])
        sc.add("vector", (lambda t2: lambda v: v.tensor_copy(
            Vtm[:, t2, :], ps_v[:, :]))(tc), waits=[(cPE, cPE.n)],
            incs=[(cVE, 1)])
        v_evac_prev = cVE.n
    sc.run(nc)
    ps1.close()

    if K_PHASES < 2:
        _emit_dbg()
        ctx.close()
        return nc
    ps2 = ExitStack()
    ps_sc = ps2.enter_context(nc.psum_tensor("ps_sc", [128, 2048], F32))
    ps_ta = ps2.enter_context(nc.psum_tensor("ps_ta", [128, 512], BF16))
    ps_tb = ps2.enter_context(nc.psum_tensor("ps_tb", [128, 512], BF16))
    ps_zt = ps2.enter_context(nc.psum_tensor("ps_zt", [128, 512], F32))

    sc = Sched(_prev_inc)

    gp_ms_at = []
    for g in range(NG):
        def mk_ms(g2):
            def f(gp):
                ii = None
                for slot in range(4):
                    sca = 4 * g2 + slot
                    for tb in range(sca + 1, 4 * g2 + 4):
                        ii = gp.memset(PTg[:, tb, slot * 128:(slot + 1) * 128], 0.0)
                assert ii is not None
                return ii
            return f
        sc.add("gpsimd", mk_ms(g), incs=[(cGC, 1)])
        gp_ms_at.append(cGC.n)

    evF_at, evD_at, mx_at, exp_at, nm_at = {}, {}, {}, {}, {}
    trpk_at, trev_at, scores_at = {}, {}, {}
    zt_ev_prev = [None]
    pack_hist = {}

    def prev_key(g, h, s_):
        if s_ > 0:
            return (g, h, s_ - 1)
        fh = g * 16 + h
        if fh == 0:
            return None
        pg, ph = divmod(fh - 1, 16)
        return (pg, ph, 3)

    def emit_scores(g, h, s_):
        t_used = (4 * g + s_ + 1) * 128
        c, p0 = h // 2, (h % 2) * 64
        w = []
        pk = prev_key(g, h, s_)
        if pk is not None:
            w.append((cSA, evF_at[pk]))
            w.append((cVE, evD_at[pk]))

        def f(t):
            ii = None
            for tcol in range((t_used + 511) // 512):
                lo = tcol * 512
                hi = min(t_used, lo + 512)
                ii = t.matmul(
                    ps_sc[:, lo:hi],
                    QT[p0:p0 + 64, c, (4 * g + s_) * 128:(4 * g + s_ + 1) * 128],
                    KT[p0:p0 + 64, c, lo:hi],
                    start=True, stop=True,
                    tile_position=((p0, 0) if p0 else None))
            return ii

        sc.add("tensor", f, waits=w, incs=[(cPE, 1)])
        scores_at[(g, h, s_)] = cPE.n

        sfull = (4 * g + s_) * 128
        sc.add("vector", (lambda sf: lambda v: v.tensor_add(
            P_sb[0:128, sf:sf + 128], ps_sc[:, sf:sf + 128], madd[:, :]))(sfull),
            waits=[(cPE, scores_at[(g, h, s_)])], incs=[(cVE, 1)])
        evD_at[(g, h, s_)] = cVE.n

        if sfull > 0:
            sc.add("scalar", (lambda sf: lambda s: s.copy(
                P_sb[0:128, 0:sf], ps_sc[:, 0:sf]))(sfull),
                waits=[(cPE, scores_at[(g, h, s_)]),
                       (cVE, evD_at[(g, h, s_)])], incs=[(cSA, 1)])
        evF_at[(g, h, s_)] = cSA.n

        sc.add("vector", (lambda tu: lambda v: v.reduce_max(
            smax[:, :], P_sb[0:128, 0:tu], AX.X, negate=True))(t_used),
            waits=[(cSA, evF_at[(g, h, s_)]), (cVE, evD_at[(g, h, s_)])],
            incs=[(cVE, 1)])
        mx_at[(g, h, s_)] = cVE.n

    def emit_tail(g, h, s_):
        t_used = (4 * g + s_ + 1) * 128
        pk = prev_key(g, h, s_)
        w = [(cVE, mx_at[(g, h, s_)])]
        if pk is not None:
            w.append((cPE, trpk_at[pk]))
        sc.add("scalar", (lambda tu: lambda s: s.activation(
            P_bf[0:128, 0:tu], P_sb[0:128, 0:tu], AF.Exp,
            bias=smax[:, 0:1], scale=1.0, accum_out=ssum[:, :]))(t_used),
            waits=w, incs=[(cSA, 1)])
        exp_at[(g, h, s_)] = cSA.n

        sc.add("vector", lambda v: v.reciprocal(srr[:, :], ssum[:, :]),
               waits=[(cSA, exp_at[(g, h, s_)])], incs=[(cVE, 1)])
        sc.add("vector", (lambda tu: lambda v: v.tensor_scalar_mul(
            P_bf[0:128, 0:tu], P_bf[0:128, 0:tu], srr[:, 0:1]))(t_used),
            incs=[(cVE, 1)])
        nm_at[(g, h, s_)] = cVE.n

    def emit_transposes(g, h, s_):
        sc_abs = 4 * g + s_
        slot = s_
        packs = [(tb0, min(4, sc_abs + 1 - tb0)) for tb0 in range(0, sc_abs + 1, 4)]
        last_pe = None
        ev_marks = []
        for pi, (tb0, cnt) in enumerate(packs):
            ps = ps_ta if pi % 2 == 0 else ps_tb
            w = [(cVE, nm_at[(g, h, s_)])]
            pu = pack_hist.get(id(ps))
            if pu is not None:
                w.append(pu)

            def mk_tr(tb0_, cnt_, ps_):
                def f(t):
                    ii = None
                    for j in range(cnt_):
                        ii = t.transpose(
                            ps_[:, j * 128:(j + 1) * 128],
                            P_bf[0:128, (tb0_ + j) * 128:(tb0_ + j + 1) * 128],
                            identb[:, :])
                    return ii
                return f

            sc.add("tensor", mk_tr(tb0, cnt, ps), waits=w, incs=[(cPE, 1)])
            last_pe = cPE.n

            if pi % 2 == 0:
                def mk_evs(tb0_, cnt_, ps_, sl_):
                    def f(s):
                        return s.copy(
                            PTg[:, tb0_:tb0_ + cnt_, sl_ * 128:(sl_ + 1) * 128],
                            ps_[:, 0:cnt_ * 128].rearrange("p (a b) -> p a b",
                                                           a=cnt_))
                    return f
                sc.add("scalar", mk_evs(tb0, cnt, ps, slot),
                       waits=[(cPE, last_pe)], incs=[(cSA, 1)])
                pack_hist[id(ps)] = (cSA, cSA.n)
                ev_marks.append((cSA, cSA.n))
            else:
                def mk_evv(tb0_, cnt_, ps_, sl_):
                    def f(v):
                        return v.tensor_copy(
                            PTg[:, tb0_:tb0_ + cnt_, sl_ * 128:(sl_ + 1) * 128],
                            ps_[:, 0:cnt_ * 128].rearrange("p (a b) -> p a b",
                                                           a=cnt_))
                    return f
                sc.add("vector", mk_evv(tb0, cnt, ps, slot),
                       waits=[(cPE, last_pe)], incs=[(cVE, 1)])
                pack_hist[id(ps)] = (cVE, cVE.n)
                ev_marks.append((cVE, cVE.n))
        trpk_at[(g, h, s_)] = last_pe
        trev_at[(g, h, s_)] = ev_marks

    def emit_av(g, h):
        ntb = 4 * g + 4
        p0 = (h % 2) * 64
        c = h // 2
        w = list(trev_at[(g, h, 3)]) + [(cGC, gp_ms_at[g])]
        for s2_ in range(3):
            w += list(trev_at[(g, h, s2_)])
        if zt_ev_prev[0] is not None:
            w.append((cSA, zt_ev_prev[0]))

        def f(t):
            ii = None
            for tb in range(ntb):
                ii = t.matmul(ps_zt[p0:p0 + 64, :],
                              Vtm[:, tb, h * 64:(h + 1) * 64],
                              PTg[:, tb, :],
                              start=(tb == 0), stop=(tb == ntb - 1),
                              tile_position=((0, p0) if p0 else None))
            return ii

        sc.add("tensor", f, waits=w, incs=[(cPE, 1)])
        sc.add("scalar", (lambda g2, p02, c2: lambda s: s.copy(
            ZT[p02:p02 + 64, c2, g2 * 512:(g2 + 1) * 512],
            ps_zt[p02:p02 + 64, :]))(g, p0, c),
            waits=[(cPE, cPE.n)], incs=[(cSA, 1)])
        zt_ev_prev[0] = cSA.n

    K_B2 = int(os.environ.get("K_B2", str(NG * NH)))
    K_TR = int(os.environ.get("K_TR", "1"))
    K_AV = int(os.environ.get("K_AV", "1"))
    cv_jobs = []
    for e_ in range(EX):
        for dc_ in range(NDC):
            for fh_ in range(2):
                cv_jobs.append((w1_h[e_, dc_ * 128:(dc_ + 1) * 128,
                                     fh_ * 2048:(fh_ + 1) * 2048],
                                w1b_h[e_, dc_ * 128:(dc_ + 1) * 128,
                                      fh_ * 2048:(fh_ + 1) * 2048]))
        for rc_ in range(FF // 256):
            cv_jobs.append((
                w2_h[e_, rc_ * 256:(rc_ + 1) * 256, :]
                .rearrange("(a p) d -> p a d", p=128),
                w2b_h[e_, rc_ * 256:(rc_ + 1) * 256, :]
                .rearrange("(a p) d -> p a d", p=128)))
    cv_out_at = {}

    def emit_cv(i):
        src, dst = cv_jobs[i]
        n = int(np.prod(src.shape[1:]))
        sl = i % 4
        w = [(cCVO[sl], cv_out_at[i - 4])] if i >= 4 else []
        sc.add("gpsimd", (lambda s2, sl2, n2: lambda g: g.dma_start(
            cv_buf[sl2][:, 0:n2], s2))(src, sl, n), waits=w,
            incs=[(cCVI[sl], 16)])
        sc.add("sync", (lambda d2, sl2, n2: lambda s: s.dma_start(
            d2, cv_buf[sl2][:, 0:n2]))(dst, sl, n),
            waits=[(cCVI[sl], cCVI[sl].n)], incs=[(cCVO[sl], 16)])
        cv_out_at[i] = cCVO[sl].n

    for i_ in range(len(cv_jobs)):
        emit_cv(i_)
    cv_final = [(c_, c_.n) for c_ in cCVO]

    prev_av = None
    ghs = [(g, h) for g in range(NG) for h in range(NH)][:K_B2]
    for g, h in ghs:
        emit_scores(g, h, 0)
        if prev_av is not None and K_AV:
            emit_av(*prev_av)
        emit_tail(g, h, 0)
        for s_ in range(1, 4):
            emit_scores(g, h, s_)
            if K_TR:
                emit_transposes(g, h, s_ - 1)
            emit_tail(g, h, s_)
        if K_TR:
            emit_transposes(g, h, 3)
        prev_av = (g, h) if K_TR else None
    if prev_av is not None and K_AV:
        emit_av(*prev_av)
    sc.run(nc)
    ps2.close()

    if K_PHASES < 3:
        _emit_dbg()
        ctx.close()
        return nc
    ps3 = ExitStack()
    ps_o = ps3.enter_context(nc.psum_tensor("ps_o", [128, 2, 1024], F32))

    sc = Sched(_prev_inc)

    wo_cast_at = []
    for c in range(NHC):
        w = [(cVE, wo_cast_at[c - 4])] if c >= 4 else []
        sc.add("sync", (lambda c2: lambda s: s.dma_start(
            wstage[:, c2 % 4, :], wo_p[c2 * 128:(c2 + 1) * 128, :]))(c),
            waits=w, incs=[(cWS[c % 4], 16)])
        sc.add("vector", (lambda c2: lambda v: v.tensor_copy(
            Wo_bf[:, c2, :], wstage[:, c2 % 4, :]))(c),
            waits=[(cWS[c % 4], cWS[c % 4].n)], incs=[(cVE, 1)])
        wo_cast_at.append(cVE.n)

    def ln_seq(src_par, dst_ap, tc_norm_prev):
        w = [(cSA, tc_norm_prev)] if tc_norm_prev is not None else []
        sc.add("vector", (lambda p2: lambda v: v.reduce_sum(
            lsum[:, :], s1buf[:, p2, :], AX.X))(src_par), waits=w, incs=[(cVE, 1)])
        sc.add("vector", (lambda p2: lambda v: v.scalar_tensor_tensor(
            s1buf[:, 1 - p2, :], s1buf[:, p2, :], 1.0, s1buf[:, p2, :],
            ALU.mult, ALU.mult, accum_out=lsq[:, :]))(src_par), incs=[(cVE, 1)])
        sc.add("vector", lambda v: v.tensor_scalar_mul(lmean[:, :], lsum[:, :],
                                                       1.0 / Dm), incs=[(cVE, 1)])
        sc.add("vector", lambda v: v.tensor_scalar_mul(lvar[:, :], lsq[:, :],
                                                       1.0 / Dm), incs=[(cVE, 1)])
        sc.add("vector", lambda v: v.tensor_mul(lm2[:, :], lmean[:, :], lmean[:, :]),
               incs=[(cVE, 1)])
        sc.add("vector", lambda v: v.tensor_sub(lvar[:, :], lvar[:, :], lm2[:, :]),
               incs=[(cVE, 1)])
        stats_done = cVE.n
        sc.add("scalar", lambda s: s.activation(llog[:, :], lvar[:, :], AF.Ln,
                                                bias=epsb[:, 0:1], scale=1.0),
               waits=[(cVE, stats_done)], incs=[(cSA, 1)])
        sc.add("scalar", lambda s: s.activation(lrstd[:, :], llog[:, :], AF.Exp,
                                                bias=0.0, scale=-0.5),
               incs=[(cSA, 1)])
        rstd_done = cSA.n
        sc.add("vector", lambda v: v.scalar_tensor_tensor(
            lnmb[:, :], lmean[:, :], -1.0, lrstd[:, :], ALU.mult, ALU.mult),
            waits=[(cSA, rstd_done)], incs=[(cVE, 1)])
        nmb_done = cVE.n
        sc.add("scalar", (lambda p2, d2: lambda s: s.activation(
            d2, s1buf[:, p2, :], AF.Identity, bias=lnmb[:, 0:1],
            scale=lrstd[:, 0:1]))(src_par, dst_ap),
            waits=[(cVE, nmb_done)], incs=[(cSA, 1)])
        return cSA.n

    xr_at = {}

    def emit_xr(tc, w):
        sc.add("sync", (lambda t2: lambda s: s.dma_start(
            xstage[:, t2 % 2, :], x_t[t2]))(tc), waits=w,
            incs=[(cXS[tc % 2], 16)])
        xr_at[tc] = cXS[tc % 2].n

    emit_xr(0, [])
    emit_xr(1, [])
    s1_tt_at = {}
    norm_prev = None
    for tc in range(NTC):
        par = tc % 2
        w = [(cVE, wo_cast_at[-1])]
        if tc >= 2:
            w.append((cVE, s1_tt_at[tc - 2]))

        def mk_mm(t2, p2):
            def f(t):
                ii = None
                for half in range(2):
                    sl = ps_o[:, p2, half * 512:(half + 1) * 512]
                    for c in range(NHC):
                        ii = t.matmul(sl, ZT[:, c, t2 * 128:(t2 + 1) * 128],
                                      Wo_bf[:, c, half * 512:(half + 1) * 512],
                                      start=(c == 0), stop=False)
                    ii = t.matmul(sl, ones_row[:, :],
                                  bo_row[:, half * 512:(half + 1) * 512],
                                  start=False, stop=True)
                return ii
            return f

        sc.add("tensor", mk_mm(tc, par), waits=w, incs=[(cPE, 1)])
        sc.add("vector", (lambda t2, p2: lambda v: v.tensor_add(
            s1buf[:, t2 % 2, :], ps_o[:, p2, :], xstage[:, t2 % 2, :]))(tc, par),
            waits=[(cPE, cPE.n), (cXS[tc % 2], xr_at[tc])], incs=[(cVE, 1)])
        s1_tt_at[tc] = cVE.n
        if tc + 2 < NTC:
            emit_xr(tc + 2, [(cVE, s1_tt_at[tc])])
        norm_prev = ln_seq(par, x1[:, tc, :], norm_prev)
    sc.run(nc)
    ps3.close()

    if K_PHASES < 4:
        _emit_dbg()
        ctx.close()
        return nc
    out_at = {}
    w1fh_dma = {}
    htmm_fh = {}

    for tt in range(NTT):
        ps4a = ExitStack()
        ps_x1a = ps4a.enter_context(nc.psum_tensor(f"ps_x1a{tt}", [128, 512], BF16))
        ps_x1b = ps4a.enter_context(nc.psum_tensor(f"ps_x1b{tt}", [128, 512], BF16))
        ps_r = ps4a.enter_context(nc.psum_tensor(f"ps_r{tt}", [128, 8], F32))
        ps_mt = ps4a.enter_context(nc.psum_tensor(f"ps_mt{tt}", [8, 128], F32))
        ps_b2 = ps4a.enter_context(nc.psum_tensor(f"ps_b2{tt}", [128, 1024], F32))

        sc = Sched(_prev_inc)

        def w1q_stream(fhg, waits=()):
            r = fhg % 4
            e2, hh = divmod(fhg, 8)
            f0 = hh * 512
            sc.add("sync", (lambda e3, f3, r3: lambda s: s.dma_start(
                W1R[:, r3, :, :], w1b_p[e3, :, :, f3:f3 + 512]))(e2, f0, r),
                waits=list(waits), incs=[(cWF[r], 16)])
            w1fh_dma[(tt, fhg)] = cWF[r].n

        for fhg in (0, 1):
            w = list(cv_final) if tt == 0 and fhg == 0 else []
            w1q_stream(fhg, w)

        xt1_evac = {}
        rtr_done = {}
        mprev_expdone = None
        mt_prev_ev = None
        b2_prev_ev = None
        for tc4 in range(4):
            tc = tt * 4 + tc4
            w = []
            if tc4 >= 2:
                w.append((cSA, xt1_evac[(tc4 - 2, 1)]))
            sc.add("vector", (lambda t2, q2: lambda v: v.tensor_copy(
                xbstage[:, q2 % 2, :], x1[:, t2, :]))(tc, tc4),
                waits=w, incs=[(cVE, 1)])
            cast_mark = cVE.n
            for pack in range(2):
                ps = ps_x1a if pack == 0 else ps_x1b
                w = [(cVE, cast_mark)]
                if tc4 > 0:
                    w.append((cSA, xt1_evac[(tc4 - 1, pack)]))

                def mk_tr(q2, p2, ps2):
                    def f(t):
                        ii = None
                        for j in range(4):
                            dc = p2 * 4 + j
                            ii = t.transpose(
                                ps2[:, j * 128:(j + 1) * 128],
                                xbstage[:, q2 % 2, dc * 128:(dc + 1) * 128],
                                identb[:, :])
                        return ii
                    return f

                sc.add("tensor", mk_tr(tc4, pack, ps), waits=w, incs=[(cPE, 1)])

                def mk_ev(q2, p2, ps2):
                    def f(s):
                        return s.copy(
                            XT1[:, p2 * 4:(p2 + 1) * 4, q2 * 128:(q2 + 1) * 128],
                            ps2[:, :].rearrange("p (a b) -> p a b", a=4))
                    return f

                sc.add("scalar", mk_ev(tc4, pack, ps), waits=[(cPE, cPE.n)],
                       incs=[(cSA, 1)])
                xt1_evac[(tc4, pack)] = cSA.n

            w = [(cSA, xt1_evac[(tc4, 1)])]
            if mprev_expdone is not None:
                w.append((cSA, mprev_expdone))

            def mk_rmm(q2):
                def f(t):
                    ii = None
                    for dc in range(NDC):
                        ii = t.matmul(ps_r[:, :],
                                      XT1[:, dc, q2 * 128:(q2 + 1) * 128],
                                      wr_bf[:, dc, :],
                                      start=(dc == 0), stop=(dc == NDC - 1))
                    return ii
                return f

            sc.add("tensor", mk_rmm(tc4), waits=w, incs=[(cPE, 1)])
            sc.add("vector", lambda v: v.reduce_max(smax[:, :], ps_r[:, :], AX.X,
                                                    negate=True),
                   waits=[(cPE, cPE.n)], incs=[(cVE, 1)])
            sc.add("scalar", lambda s: s.activation(
                probs[:, :], ps_r[:, :], AF.Exp, bias=smax[:, 0:1], scale=1.0,
                accum_out=ssum[:, :]), waits=[(cVE, cVE.n)], incs=[(cSA, 1)])
            mprev_expdone = cSA.n
            sc.add("vector", lambda v: v.reciprocal(srr[:, :], ssum[:, :]),
                   waits=[(cSA, cSA.n)], incs=[(cVE, 1)])
            sc.add("vector", lambda v: v.tensor_scalar_mul(probs[:, :], probs[:, :],
                                                           srr[:, 0:1]),
                   incs=[(cVE, 1)])
            sc.add("vector", lambda v: v.reduce_max(pmax[:, :], probs[:, :], AX.X),
                   incs=[(cVE, 1)])
            sc.add("vector", (lambda q2: lambda v: v.scalar_tensor_tensor(
                Mw[:, q2, :], probs[:, :], pmax[:, 0:1], probs[:, :],
                ALU.is_equal, ALU.mult))(tc4), incs=[(cVE, 1)])
            m_done = cVE.n

            w = [(cVE, m_done)]
            if mt_prev_ev is not None:
                w.append((cSA, mt_prev_ev))
            sc.add("tensor", (lambda q2: lambda t: t.transpose(
                ps_mt[:, :], Mw[:, q2, :], identf[:, :]))(tc4),
                waits=w, incs=[(cPE, 1)])
            sc.add("scalar", (lambda q2: lambda s: s.copy(
                MT_bf[0:8, q2 * 128:(q2 + 1) * 128], ps_mt[:, :]))(tc4),
                waits=[(cPE, cPE.n)], incs=[(cSA, 1)])
            mt_prev_ev = cSA.n

            w = [(cSA, mt_prev_ev)]
            if b2_prev_ev is not None:
                w.append((cVE, b2_prev_ev))
            if tt > 0:
                w.append((cOUTs[tc4], out_at[(tt - 1, tc4)]))

            def mk_b2(q2):
                def f(t):
                    ii = None
                    for half in range(2):
                        ii = t.matmul(ps_b2[:, half * 512:(half + 1) * 512],
                                      MT_bf[0:8, q2 * 128:(q2 + 1) * 128],
                                      b2s[0:8, half * 512:(half + 1) * 512],
                                      start=True, stop=True)
                    return ii
                return f

            sc.add("tensor", mk_b2(tc4), waits=w, incs=[(cPE, 1)])
            sc.add("vector", (lambda q2: lambda v: v.tensor_copy(
                acc[:, q2, :], ps_b2[:, :]))(tc4), waits=[(cPE, cPE.n)],
                incs=[(cVE, 1)])
            b2_prev_ev = cVE.n
        sc.run(nc)
        ps4a.close()

        ps4b = ExitStack()
        ps_all = ps4b.enter_context(nc.psum_tensor(f"ps_all{tt}", [128, 4096], F32))
        sc = Sched(_prev_inc)

        ht_ev_at = {}
        y_ev_at = {}
        w2dma_at = {}
        ymm_at = {}
        w2_ymm_hist = {}

        def w1q_stream(fhg):
            r = fhg % 4
            e2, hh = divmod(fhg, 8)
            f0 = hh * 512
            w = []
            if fhg - 4 >= 0:
                w.append((cPE, htmm_fh[(tt, fhg - 4)]))
            sc.add("sync", (lambda e3, f3, r3: lambda s: s.dma_start(
                W1R[:, r3, :, :], w1b_p[e3, :, :, f3:f3 + 512]))(e2, f0, r),
                waits=w, incs=[(cWF[r], 16)])
            w1fh_dma[(tt, fhg)] = cWF[r].n

        W2SLOT_BF = [W2bf[:, 0, :], W2bf[:, 1, :], W2bf2[:, 0, :], W2bf2[:, 1, :]]

        def w2_dma(e, fc, extra=()):
            r = fc % 4
            w = list(extra)
            key = (e, fc - 4) if fc >= 4 else ((e - 1, 28 + fc) if e > 0 else None)
            if key is not None and key in w2_ymm_hist:
                w.append((cPE, w2_ymm_hist[key]))
            sc.add("scalar", (lambda e2, f2, r2: lambda s: s.dma_start(
                W2SLOT_BF[r2],
                w2b_h[e2, f2 * 128:(f2 + 1) * 128, :]))(e, fc, r),
                waits=w, incs=[(cW2R[r], 16)])
            w2dma_at[(e, fc)] = cW2R[r].n

        for j in range(3):
            w2_dma(0, j, list(cv_final) if (tt == 0 and j == 0) else ())

        for e in range(EX):
            for fc in range(NFC):
                fhg = e * 8 + fc // 4
                if fc % 4 == 0:
                    if fhg + 2 <= EX * 8 - 1:
                        w1q_stream(fhg + 2)
                bank = fc % 4
                r_ = (e * 8 + fc // 4) % 4
                w = [(cWF[r_], w1fh_dma[(tt, fhg)])]
                if (e, fc - 4) in ht_ev_at:
                    w.append((cSA, ht_ev_at[(e, fc - 4)]))
                elif fc < 4 and e > 0:
                    w.append((cSA, ht_ev_at[(e - 1, 28 + fc)]))
                if fc < 4 and e > 0:
                    w.append((cVE, y_ev_at[(e - 1, 0 if fc < 2 else 1)]))

                def mk_ht(e2, f2, g2, b2_):
                    def f(t):
                        ii = None
                        for dc in range(NDC):
                            ii = t.matmul(
                                ps_all[:, b2_ * 512:(b2_ + 1) * 512],
                                W1R[:, g2 % 4, dc, (f2 % 4) * 128:(f2 % 4 + 1) * 128],
                                XT1[:, dc, :],
                                start=(dc == 0), stop=(dc == NDC - 1))
                        return ii
                    return f

                sc.add("tensor", mk_ht(e, fc, fhg, bank), waits=w, incs=[(cPE, 1)])
                htmm_fh[(tt, fhg)] = cPE.n

                def mk_htev(e2, f2, b2_):
                    def f(s):
                        return s.activation(
                            HT[:, f2, :], ps_all[:, b2_ * 512:(b2_ + 1) * 512],
                            AF.Relu, bias=b1s[:, e2, f2:f2 + 1], scale=1.0)
                    return f

                sc.add("scalar", mk_htev(e, fc, bank), waits=[(cPE, cPE.n)],
                       incs=[(cSA, 1)])
                ht_ev_at[(e, fc)] = cSA.n

            for fc in range(NFC):
                w = [(cW2R[fc % 4], w2dma_at[(e, fc)])]
                if fc == 0:
                    w.append((cSA, ht_ev_at[(e, NFC - 1)]))
                    for q in range(1, 4):
                        if e > 0:
                            w.append((cVE, y_ev_at[(e - 1, q)]))

                def mk_y(e2, f2):
                    def f(t):
                        ii = None
                        rhs = W2SLOT_BF[f2 % 4]
                        for q in range(4):
                            for half in range(2):
                                ii = t.matmul(
                                    ps_all[:, q * 1024 + half * 512:
                                           q * 1024 + (half + 1) * 512],
                                    HT[:, f2, q * 128:(q + 1) * 128],
                                    rhs[:, half * 512:(half + 1) * 512],
                                    start=(f2 == 0), stop=(f2 == NFC - 1))
                        return ii
                    return f

                sc.add("tensor", mk_y(e, fc), waits=w, incs=[(cPE, 1)])
                w2_ymm_hist[(e, fc)] = cPE.n
                nd = (e, fc + 3) if fc + 3 < NFC else (e + 1, fc + 3 - NFC)
                if nd[0] < EX and nd not in w2dma_at:
                    w2_dma(*nd)
            ymm_at[e] = cPE.n

            for q in range(4):
                sc.add("vector", (lambda e2, q2: lambda v: v.scalar_tensor_tensor(
                    acc[:, q2, :], ps_all[:, q2 * 1024:(q2 + 1) * 1024],
                    Mw[:, q2, e2:e2 + 1], acc[:, q2, :],
                    ALU.mult, ALU.add))(e, q),
                    waits=[(cPE, ymm_at[e])], incs=[(cVE, 1)])
                y_ev_at[(e, q)] = cVE.n

        norm_prev2 = None
        for q in range(4):
            tc = tt * 4 + q
            sc.add("vector", (lambda t2, q2: lambda v: v.tensor_add(
                s1buf[:, q2 % 2, :], acc[:, q2, :], x1[:, t2, :]))(tc, q),
                waits=[(cVE, y_ev_at[(EX - 1, q)])], incs=[(cVE, 1)])

            w = [(cSA, norm_prev2)] if norm_prev2 is not None else []
            sc.add("vector", (lambda q2: lambda v: v.reduce_sum(
                lsum[:, :], s1buf[:, q2 % 2, :], AX.X))(q), waits=w, incs=[(cVE, 1)])
            sc.add("vector", (lambda q2: lambda v: v.scalar_tensor_tensor(
                s1buf[:, 1 - q2 % 2, :], s1buf[:, q2 % 2, :], 1.0,
                s1buf[:, q2 % 2, :], ALU.mult, ALU.mult,
                accum_out=lsq[:, :]))(q), incs=[(cVE, 1)])
            sc.add("vector", lambda v: v.tensor_scalar_mul(lmean[:, :], lsum[:, :],
                                                           1.0 / Dm), incs=[(cVE, 1)])
            sc.add("vector", lambda v: v.tensor_scalar_mul(lvar[:, :], lsq[:, :],
                                                           1.0 / Dm), incs=[(cVE, 1)])
            sc.add("vector", lambda v: v.tensor_mul(lm2[:, :], lmean[:, :],
                                                    lmean[:, :]), incs=[(cVE, 1)])
            sc.add("vector", lambda v: v.tensor_sub(lvar[:, :], lvar[:, :],
                                                    lm2[:, :]), incs=[(cVE, 1)])
            stats_done = cVE.n
            sc.add("scalar", lambda s: s.activation(llog[:, :], lvar[:, :], AF.Ln,
                                                    bias=epsb[:, 0:1], scale=1.0),
                   waits=[(cVE, stats_done)], incs=[(cSA, 1)])
            sc.add("scalar", lambda s: s.activation(lrstd[:, :], llog[:, :], AF.Exp,
                                                    bias=0.0, scale=-0.5),
                   incs=[(cSA, 1)])
            sc.add("vector", lambda v: v.scalar_tensor_tensor(
                lnmb[:, :], lmean[:, :], -1.0, lrstd[:, :], ALU.mult, ALU.mult),
                waits=[(cSA, cSA.n)], incs=[(cVE, 1)])
            sc.add("scalar", (lambda q2: lambda s: s.activation(
                acc[:, q2, :], s1buf[:, q2 % 2, :], AF.Identity, bias=lnmb[:, 0:1],
                scale=lrstd[:, 0:1]))(q), waits=[(cVE, cVE.n)], incs=[(cSA, 1)])
            norm_prev2 = cSA.n
            sc.add("sync", (lambda t2, q2: lambda s: s.dma_start(
                out_t[t2], acc[:, q2, :]))(tc, q), waits=[(cSA, norm_prev2)],
                incs=[(cOUTs[q], 16)])
            out_at[(tt, q)] = cOUTs[q].n

        if tt == NTT - 1:
            sc.add("gpsimd", lambda g: g.memset(epsb[:, :], 1e-5),
                   waits=[(c_, c_.n) for c_ in cOUTs])
        sc.run(nc)
        ps4b.close()

    _emit_dbg()
    ctx.close()
    return nc


_NC_CACHE = None


def kernel(**inputs):
    global _NC_CACHE
    order = ["L0_attn_residual", "L0_attn_Wq", "L0_attn_Wk", "L0_attn_Wv",
             "L0_attn_Wo", "L0_attn_bq", "L0_attn_bk", "L0_attn_bv",
             "L0_attn_bo", "L0_ffn_w_router", "L0_ffn_w1", "L0_ffn_b1",
             "L0_ffn_w2", "L0_ffn_b2"]
    a = {k: np.ascontiguousarray(np.asarray(inputs[k], dtype=np.float32))
         for k in order}
    if _NC_CACHE is None:
        _NC_CACHE = build_nc()
    nc = _NC_CACHE
    shared = {
        "wq": a["L0_attn_Wq"], "wk": a["L0_attn_Wk"], "wv": a["L0_attn_Wv"],
        "wo": a["L0_attn_Wo"], "bq": a["L0_attn_bq"], "bk": a["L0_attn_bk"],
        "bv": a["L0_attn_bv"], "bo": a["L0_attn_bo"].reshape(1, Dm),
        "wr": a["L0_ffn_w_router"], "w1": a["L0_ffn_w1"], "b1": a["L0_ffn_b1"],
        "w2": a["L0_ffn_w2"], "b2": a["L0_ffn_b2"],
    }
    x = a["L0_attn_residual"]
    in_maps = [dict(shared, x=np.ascontiguousarray(x[c])) for c in range(8)]
    res = run_bass_kernel_spmd(nc, in_maps, core_ids=list(range(8)))
    return np.stack([np.asarray(res.results[c]["out"]) for c in range(8)], axis=0)


if __name__ == "__main__":
    nc = build_nc()
    print("built ok, instructions:", nc.n_instructions())
